# version 16 (final)
# speedup vs baseline: 1.3760x; 1.0231x over previous
import numpy as np
import ml_dtypes
from contextlib import ExitStack
import concourse.bass as bass
import concourse.mybir as mybir
from concourse.alu_op_type import AluOpType as Op
from concourse.bass_utils import run_bass_kernel_spmd
from concourse.aot_env import allow_ici_all_to_all

F32 = mybir.dt.float32
BF16 = mybir.dt.bfloat16
FP16 = mybir.dt.float16
AF = mybir.ActivationFunctionType
BF = ml_dtypes.bfloat16

B, S, E, H, D = 2, 4096, 4096, 32, 128
KVH = 8
NCORE = 8
TOK = 1024
NT = S // 128
NE = E // 128
NR = 820
NEGBIG = -30000.0


def ceil_div(a, b):
    return (a + b - 1) // b


class Sem:
    def __init__(self, handle):
        self.h = handle
        self.n = 0

    def inc(self, instr, k=1):
        instr.then_inc(self.h, k)
        self.n += k
        return self.n


class Emitter:

    def __init__(self, block, nc):
        self.block = block
        self.nc = nc

    def sync(self, fn): fn(self.nc.sync)
    def gpsimd(self, fn): fn(self.nc.gpsimd)
    def tensor(self, fn): fn(self.nc.tensor)
    def scalar(self, fn): fn(self.nc.scalar)
    def vector(self, fn): fn(self.nc.vector)



def host_prepare(inp):
    cs = []
    invf = 1.0 / (10000.0 ** (np.arange(0, D, 2, dtype=np.float32) / D))
    fr = np.outer(np.arange(S, dtype=np.float32), invf)
    emb = np.concatenate([fr, fr], -1)
    cos_ds = np.ascontiguousarray(np.cos(emb).T).astype(BF)
    sin_ds = np.ascontiguousarray(np.sin(emb).T).astype(BF)
    cos_sd = np.ascontiguousarray(np.cos(emb)).astype(BF)
    sin_sd = np.ascontiguousarray(np.sin(emb)).astype(BF)
    ex8 = np.zeros((8, 128), np.float32)
    for e_ in range(8):
        ex8[e_, e_ * 16:(e_ + 1) * 16] = 1.0
    ex8 = ex8.astype(BF)
    idm = np.eye(128, dtype=BF)
    triu = np.triu(np.full((128, 128), NEGBIG, np.float32), k=1).astype(BF)
    triu01 = np.tril(np.ones((128, 128), np.float32)).astype(BF)
    x = inp["L0_attn_hidden_states"]

    def T(a):
        return np.ascontiguousarray(np.asarray(a).T.astype(np.float32))

    woT = T(inp["L0_attn_Wo"])
    wbT = T(inp["L0_ffn_base_w"])
    wgT = T(inp["L0_ffn_gate_w"])
    bb = np.ascontiguousarray(
        inp["L0_ffn_base_b"].reshape(NE, 128).T.astype(np.float32))
    lAT = np.ascontiguousarray(
        np.transpose(inp["L0_ffn_lora_A"], (0, 2, 1)).astype(np.float32))
    lB = inp["L0_ffn_lora_B"]
    lBstack = np.zeros((128, 4096), np.float32)
    for e_ in range(8):
        lBstack[e_ * 16:(e_ + 1) * 16, :] = lB[e_].T
    w1oT = T(inp["L1_attn_Wo"])
    w1V = np.ascontiguousarray(inp["L1_ffn_V"].astype(np.float32))
    w1UT = T(inp["L1_ffn_U"])
    s1 = np.ascontiguousarray(
        inp["L1_ffn_S"].reshape(8, 128).T.astype(np.float32))
    for c in range(NCORE):
        b = c // 4
        t0 = (c % 4) * TOK
        hs = slice(4 * c * 128, (4 * c + 4) * 128)
        d = {
            "xT": np.ascontiguousarray(x[b, t0:t0 + TOK, :].T.astype(np.float32)),
            "cosds": cos_ds, "sinds": sin_ds, "idm": idm, "triu": triu,
            "triu01": triu01,
            "cossd": cos_sd, "sinsd": sin_sd, "ex8": ex8,
            "w_qT": T(inp["L0_attn_Wq"][hs]),
            "w_kT": T(inp["L0_attn_Wk"][c * 128:(c + 1) * 128]),
            "w_vT": T(inp["L0_attn_Wv"][c * 128:(c + 1) * 128]),
            "hp1": np.ascontiguousarray(
                inp["L0_attn_hash_proj1"][4 * c:4 * c + 4].reshape(512, 128).astype(np.float32)),
            "hp2": np.ascontiguousarray(
                inp["L0_attn_hash_proj2"][4 * c:4 * c + 4].reshape(512, 128).astype(np.float32)),
            "hb1": np.ascontiguousarray(
                inp["L0_attn_hash_bias1"][4 * c:4 * c + 4, 0, :].T.astype(np.float32)),
            "hb2": np.ascontiguousarray(
                inp["L0_attn_hash_bias2"][4 * c:4 * c + 4, 0, :].T.astype(np.float32)),
            "w_oT": woT, "w_bT": wbT, "w_gT": wgT, "b_b": bb,
            "w_lA": np.ascontiguousarray(np.transpose(lAT.reshape(8, NE, 128, 16), (1, 2, 0, 3)).reshape(4096, 128)), "w_lB": lBstack,
            "w1_qT": T(inp["L1_attn_Wq"][hs]),
            "w1_kT": T(inp["L1_attn_Wk"][hs]),
            "w1_vT": T(inp["L1_attn_Wv"][hs]),
            "w1_oT": w1oT, "w1_V": w1V, "w1_UT": w1UT, "w1_S": s1,
        }
        cs.append(d)
    return cs



def emit_ln(nc, E_, sems, sb, src_sb, xn_hbm):
    d, ds, sc, vc, tn = sems["dma"], sems["ds"], sems["sc"], sems["vc"], sems["tns"]
    ps_stat, sq = sb["ps_stat"], sb["sq"]
    ones_col, ones_row = sb["ones_col"], sb["ones_row"]
    mrow, mrow_bf, ps_b, mB, xn_t = (sb["mrow"], sb["mrow_bf"], sb["ps_bcast"],
                                     sb["mB"], sb["xn_t"])
    sq_gate = {}
    for t in range(NE):
        @E_.scalar
        def _(s, t=t):
            if t > 0:
                s.wait_ge(tn.h, sq_gate[t - 1])
            sc.inc(nc.scalar.activation(sq[:, :], src_sb[:, t * TOK:(t + 1) * TOK],
                                        AF.Square))
        sq_done = sc.n

        @E_.tensor
        def _(te, t=t, sq_done=sq_done):
            te.wait_ge(sc.h, sq_done)
            for ch in range(TOK // 512):
                c0, c1 = ch * 512, (ch + 1) * 512
                nc.tensor.matmul(ps_stat[:, c0:c1], ones_col[:, :],
                                 src_sb[:, t * TOK + c0:t * TOK + c1],
                                 start=(t == 0), stop=(t == NE - 1))
                ins = nc.tensor.matmul(ps_stat[:, TOK + c0:TOK + c1], ones_col[:, :],
                                       sq[:, c0:c1], start=(t == 0), stop=(t == NE - 1))
            tn.inc(ins)
        sq_gate[t] = tn.n
    mm_done = tn.n

    @E_.vector
    def _(v):
        v.wait_ge(tn.h, mm_done)
        vc.inc(nc.vector.tensor_scalar(mrow[:, 0:TOK], ps_stat[:, 0:TOK],
                                       1.0 / E, None, op0=Op.mult))
        v.wait_ge(vc.h, vc.n)
        vc.inc(nc.vector.tensor_scalar(mrow[:, TOK:], ps_stat[:, TOK:],
                                       1.0 / E, None, op0=Op.mult))
        v.wait_ge(vc.h, vc.n)
        vc.inc(nc.vector.tensor_tensor(ps_stat[:, 0:TOK], mrow[:, 0:TOK],
                                       mrow[:, 0:TOK], op=Op.mult))
        v.wait_ge(vc.h, vc.n)
        vc.inc(nc.vector.tensor_tensor(mrow[:, TOK:], mrow[:, TOK:],
                                       ps_stat[:, 0:TOK], op=Op.subtract))
    var_done = vc.n

    @E_.scalar
    def _(s):
        s.wait_ge(vc.h, var_done)
        sc.inc(nc.scalar.activation(ps_stat[:, 0:TOK], mrow[:, TOK:], AF.Sqrt,
                                    bias=sb["eps"][0:1, 0:1]))
    sd_done = sc.n

    @E_.vector
    def _(v):
        v.wait_ge(sc.h, sd_done)
        vc.inc(nc.vector.reciprocal(mrow[:, TOK:], ps_stat[:, 0:TOK]))
        v.wait_ge(vc.h, vc.n)
        vc.inc(nc.vector.tensor_copy(mrow_bf[:, :], mrow[:, :]))
    cast_done = vc.n

    @E_.tensor
    def _(te):
        te.wait_ge(vc.h, cast_done)
        for ch in range(2 * TOK // 512):
            ins = nc.tensor.matmul(ps_b[:, ch * 512:(ch + 1) * 512], ones_row[:, :],
                                   mrow_bf[:, ch * 512:(ch + 1) * 512],
                                   start=True, stop=True)
        tn.inc(ins)
    bc_done = tn.n

    @E_.vector
    def _(v):
        v.wait_ge(tn.h, bc_done)
        vc.inc(nc.vector.tensor_copy(mB[:, :], ps_b[:, :]))
    mB_done = vc.n

    out_gate = {}
    for t in range(NE):
        buf = xn_t[:, (t % 2) * TOK:(t % 2 + 1) * TOK]

        @E_.vector
        def _(v, t=t, buf=buf):
            v.wait_ge(vc.h, mB_done)
            if t >= 2:
                v.wait_ge(ds.h, out_gate[t - 2])
            vc.inc(nc.vector.tensor_tensor(buf, src_sb[:, t * TOK:(t + 1) * TOK],
                                           mB[:, 0:TOK], op=Op.subtract))
            v.wait_ge(vc.h, vc.n)
            vc.inc(nc.vector.tensor_tensor(buf, buf, mB[:, TOK:], op=Op.mult))
        norm_done = vc.n

        @E_.sync
        def _(sy, t=t, buf=buf, norm_done=norm_done):
            sy.wait_ge(vc.h, norm_done)
            ds.inc(sy.dma_start(out=xn_hbm[t * 128:(t + 1) * 128, :], in_=buf), 16)
        out_gate[t] = ds.n


def emit_gather(nc, E_, sems, src_hbm_h, dst_hbm_h, wait_dma):
    cc = sems["cc"]

    @E_.gpsimd
    def _(gp):
        gp.wait_ge(sems["dma"].h, wait_dma)
        cc.inc(gp.collective_compute(
            "AllGather", mybir.AluOpType.bypass,
            replica_groups=[list(range(NCORE))],
            ins=[src_hbm_h.ap().opt()], outs=[dst_hbm_h.ap().opt()]))
    return cc.n


def emit_alltoall(nc, E_, sems, src_hbm_h, dst_hbm_h, wait_dma):
    cc = sems["cc"]
    wd, ws = sems["dma"].n, sems["ds"].n

    @E_.gpsimd
    def _(gp):
        gp.wait_ge(sems["dma"].h, wd)
        if ws:
            gp.wait_ge(sems["ds"].h, ws)
        with allow_ici_all_to_all():
            cc.inc(gp.collective_compute(
                "AllToAll", mybir.AluOpType.bypass,
                replica_groups=[list(range(NCORE))],
                ins=[src_hbm_h.ap().opt()], outs=[dst_hbm_h.ap().opt()]))
    return cc.n


def emit_rope(nc, E_, sems, ps, cos_sl, sin_sl, out_bf, nheads, tmp):
    vc = sems["vc"]
    for h in range(nheads):
        lo = slice(h * 128, h * 128 + 64)
        hi = slice(h * 128 + 64, h * 128 + 128)

        @E_.vector
        def _(v, lo=lo, hi=hi):
            vc.inc(nc.vector.tensor_tensor(tmp[:, 0:64], ps[:, hi], sin_sl[:, 0:64], op=Op.mult))
            vc.inc(nc.vector.tensor_tensor(out_bf[:, lo], ps[:, lo], cos_sl[:, 0:64], op=Op.mult))
            v.wait_ge(vc.h, vc.n)
            vc.inc(nc.vector.tensor_tensor(out_bf[:, lo], out_bf[:, lo], tmp[:, 0:64], op=Op.subtract))
            vc.inc(nc.vector.tensor_tensor(tmp[:, 64:128], ps[:, lo], sin_sl[:, 64:128], op=Op.mult))
            vc.inc(nc.vector.tensor_tensor(out_bf[:, hi], ps[:, hi], cos_sl[:, 64:128], op=Op.mult))
            v.wait_ge(vc.h, vc.n)
            vc.inc(nc.vector.tensor_tensor(out_bf[:, hi], out_bf[:, hi], tmp[:, 64:128], op=Op.add))
    return vc.n


def emit_proj0(nc, E_, sems, sb, g_xn, qspill, core_id=None):
    d, ds, dg, tn, sc, vc = (sems["dma"], sems["ds"], sems["dg"], sems["tns"],
                             sems["sc"], sems["vc"])
    xng = sb["xng"]
    wq, wk, wv = sb["w_q_sb"], sb["w_k_sb"], sb["w_v_sb"]
    ps_q, ps_k, ps_v = sb["ps_q"], sb["ps_k"], sb["ps_v"]
    ps_tr = sb["ps_tr"]
    qr_tm = sb["qr_tm"]
    kr_tm = sb["kr_tm"]
    tr_ev = sb["tr_ev"]
    rtmp = sb["rtmp"]
    cossd, sinsd = sb["cossd"], sb["sinsd"]
    krT, vtm = sb["krT"], sb["vtm"]
    xng_gate = {}
    for b in range(B):
        for tt in range(NT):
            it = b * NT + tt
            par = it % 2
            xbuf = xng[:, par * 4096:(par + 1) * 4096]

            @E_.sync
            def _(sy, b=b, tt=tt, it=it, xbuf=xbuf):
                if it >= 2:
                    sy.wait_ge(tn.h, xng_gate[it - 2])
                src = g_xn.rearrange("(r e p) m -> r p e m", e=NE, p=128)
                rr = 4 * b + tt // 8
                co = (tt % 8) * 128
                d.inc(sy.dma_start(out=xbuf.rearrange("p (e m) -> p e m", m=128),
                                   in_=src[rr, :, :, co:co + 128]), 16)
            xn_ok = d.n

            @E_.tensor
            def _(te, it=it, xbuf=xbuf, xn_ok=xn_ok):
                te.wait_ge(d.h, xn_ok)
                if it >= 1:
                    te.wait_ge(vc.h, sems["_p0_evac"].get(it - 1, 0))
                for e in range(NE):
                    st, sp = (e == 0), (e == NE - 1)
                    nc.tensor.matmul(ps_q[:, :], xbuf[:, e * 128:(e + 1) * 128],
                                     wq[:, e * 512:(e + 1) * 512], start=st, stop=sp)
                    nc.tensor.matmul(ps_k[:, :], xbuf[:, e * 128:(e + 1) * 128],
                                     wk[:, e * 128:(e + 1) * 128], start=st, stop=sp)
                    ins = nc.tensor.matmul(ps_v[:, :], xbuf[:, e * 128:(e + 1) * 128],
                                           wv[:, e * 128:(e + 1) * 128], start=st, stop=sp)
                tn.inc(ins)
            mm_ok = tn.n
            xng_gate[it] = tn.n

            @E_.vector
            def _(v, mm_ok=mm_ok):
                v.wait_ge(tn.h, mm_ok)
            cos_sl = cossd[:, tt * 128:(tt + 1) * 128]
            sin_sl = sinsd[:, tt * 128:(tt + 1) * 128]
            emit_rope(nc, E_, sems, ps_q, cos_sl, sin_sl, qr_tm, 4, rtmp)
            emit_rope(nc, E_, sems, ps_k, cos_sl, sin_sl, kr_tm, 1, rtmp)
            rope_ok = vc.n

            @E_.vector
            def _(v, b=b, tt=tt):
                vc.inc(nc.vector.tensor_copy(
                    vtm[:, b * NT * 132 + tt * 132:b * NT * 132 + tt * 132 + 128],
                    ps_v[:, :]))
            vevac_ok = vc.n
            sems.setdefault("_p0_evac", {})[it] = vc.n

            @E_.tensor
            def _(te, rope_ok=rope_ok, it=it):
                te.wait_ge(vc.h, rope_ok)
                if it >= 1:
                    te.wait_ge(vc.h, sems["_p0_trev"].get(it - 1, 0))
                for h in range(4):
                    nc.tensor.transpose(ps_tr[:, h * 128:(h + 1) * 128],
                                        qr_tm[:, h * 128:(h + 1) * 128], sb["idm"][:, :])
                tn.inc(nc.tensor.transpose(sb["ps_trk"][:, :], kr_tm[:, :], sb["idm"][:, :]))
            tr_ok = tn.n

            @E_.vector
            def _(v, tr_ok=tr_ok, b=b, tt=tt):
                v.wait_ge(tn.h, tr_ok)
                v.wait_ge(ds.h, sems.get("_p0_sp", {}).get("spill", 0))
                vc.inc(nc.vector.tensor_copy(tr_ev[:, :], ps_tr[:, :]))
                vc.inc(nc.vector.tensor_copy(
                    krT[:, b * 4096 + tt * 128:b * 4096 + (tt + 1) * 128],
                    sb["ps_trk"][:, :]))
            trev_ok = vc.n
            sems.setdefault("_p0_trev", {})[it] = vc.n

            @E_.sync
            def _(sy, b=b, tt=tt, trev_ok=trev_ok):
                sy.wait_ge(vc.h, trev_ok)
                qsp = qspill.rearrange("(pair p) m -> pair p m", p=128)
                for h in range(4):
                    ds.inc(sy.dma_start(out=qsp[b * 4 + h, :, tt * 128:(tt + 1) * 128],
                                        in_=tr_ev[:, h * 128:(h + 1) * 128]), 16)
            sems.setdefault("_p0_sp", {})["spill"] = ds.n
    return d.n


def emit_gate_low(nc, E_, sems, sb, xn_own, core_id=0):
    d, tn, sc, vc = sems["dma"], sems["tns"], sems["sc"], sems["vc"]
    xt = sb["xn_loc"]
    wg, wa = sb["w_g_sb"], sb["w_lA_sb"]
    ps_g, ps_low = sb["ps_g"], sb["ps_low"]
    gate_d = {}
    for e in range(NE):
        par = e % 2
        xbuf = xt[:, par * 1024:(par + 1) * 1024]

        @E_.sync
        def _(sy, e=e, xbuf=xbuf):
            if e >= 2:
                sy.wait_ge(tn.h, gate_d[e - 2])
            src = xn_own.rearrange("(e p) m -> e p m", p=128)
            d.inc(sy.dma_start(out=xbuf, in_=src[e]), 16)
        ld = d.n

        @E_.tensor
        def _(te, e=e, xbuf=xbuf, ld=ld):
            te.wait_ge(d.h, ld)
            st, sp = (e == 0), (e == NE - 1)
            for ch in range(2):
                c0, c1 = ch * 512, (ch + 1) * 512
                nc.tensor.matmul(ps_g[:, c0:c1], wg[:, e * 8:(e + 1) * 8],
                                 xbuf[:, c0:c1], start=st, stop=sp)
                ins = nc.tensor.matmul(ps_low[:, c0:c1], wa[:, e * 128:(e + 1) * 128],
                                       xbuf[:, c0:c1], start=st, stop=sp)
            tn.inc(ins)
        gate_d[e] = tn.n
    low_ok = tn.n

    gl = sb["gl_sb"]
    glT = sb["glT"]
    ps_glT = sb["ps_glT"]
    mx = sb["mx8"]
    wcols = sb["wcols"]
    wf_tm = sb["wf_tm"]
    ps_wfT = sb["ps_wfT"]
    wf_fm = sb["wf_fm"]
    low_sb = sb["low_sb"]
    ps_wex = sb["ps_wex"]
    ex8 = sb["ex8"]

    @E_.vector
    def _(v):
        v.wait_ge(tn.h, low_ok)
        vc.inc(nc.vector.tensor_copy(gl[:, :], ps_g[:, :]))
        vc.inc(nc.vector.tensor_copy(low_sb[:, :], ps_low[:, :]))
    gl_ok = vc.n

    for t in range(8):
        @E_.tensor
        def _(te, t=t, gl_ok=gl_ok):
            te.wait_ge(vc.h, gl_ok)
            if t >= 1:
                te.wait_ge(vc.h, sems["_gt_ev"].get(t - 1, 0))
            tn.inc(nc.tensor.transpose(ps_glT[:, :], gl[:, t * 128:(t + 1) * 128],
                                       sb["idm_f32"][0:8, 0:8]))
        t_ok = tn.n

        @E_.vector
        def _(v, t=t, t_ok=t_ok):
            v.wait_ge(tn.h, t_ok)
            vc.inc(nc.vector.tensor_copy(glT[:, :], ps_glT[:, :]))
            v.wait_ge(vc.h, vc.n)
            vc.inc(nc.vector.max(out=mx[:, :], in_=glT[:, :]))
            v.wait_ge(vc.h, vc.n)
            vc.inc(nc.vector.tensor_tensor(wcols[:, 0:1], mx[:, 1:2], mx[:, 0:1],
                                           op=Op.subtract))
        d_ok = vc.n

        @E_.scalar
        def _(s, d_ok=d_ok):
            s.wait_ge(vc.h, d_ok)
            sc.inc(nc.scalar.activation(wcols[:, 1:2], wcols[:, 0:1], AF.Sigmoid,
                                        scale=-1.0))
        w1_ok = sc.n

        @E_.vector
        def _(v, t=t, w1_ok=w1_ok):
            v.wait_ge(sc.h, w1_ok)
            vc.inc(nc.vector.tensor_scalar(wcols[:, 2:3], wcols[:, 1:2], -1.0, 1.0,
                                           op0=Op.mult, op1=Op.add))
            v.wait_ge(vc.h, vc.n)
            vc.inc(nc.vector.tensor_scalar(wcols[:, 3:4], mx[:, 0:1], 1.0, None, op0=Op.mult))
            v.wait_ge(vc.h, vc.n)
            vc.inc(nc.vector.tensor_scalar(glT[:, :], glT[:, :], 1.0, None, op0=Op.mult))
            v.wait_ge(vc.h, vc.n)
            vc.inc(nc.vector.tensor_scalar(wf_tm[:, :], glT[:, :], mx[:, 0:1], None,
                                           op0=Op.is_equal))
            v.wait_ge(vc.h, vc.n)
            vc.inc(nc.vector.tensor_scalar(wf_tm[:, :], wf_tm[:, :], wcols[:, 1:2], 2.0,
                                           op0=Op.mult, op1=Op.mult))
            v.wait_ge(vc.h, vc.n)
            vc.inc(nc.vector.tensor_scalar(sb["wf_tm2"][:, :], glT[:, :], mx[:, 1:2], None,
                                           op0=Op.is_equal))
            v.wait_ge(vc.h, vc.n)
            vc.inc(nc.vector.tensor_scalar(sb["wf_tm2"][:, :], sb["wf_tm2"][:, :],
                                           wcols[:, 2:3], 2.0, op0=Op.mult, op1=Op.mult))
            v.wait_ge(vc.h, vc.n)
            vc.inc(nc.vector.tensor_tensor(wf_tm[:, :], wf_tm[:, :], sb["wf_tm2"][:, :],
                                           op=Op.add))
        wf_ok = vc.n

        @E_.tensor
        def _(te, t=t, wf_ok=wf_ok):
            te.wait_ge(vc.h, wf_ok)
            tn.inc(nc.tensor.transpose(ps_wfT[:, :], wf_tm[:, :], sb["idm"][:, :]))
        wfT_ok = tn.n

        @E_.vector
        def _(v, t=t, wfT_ok=wfT_ok):
            v.wait_ge(tn.h, wfT_ok)
            vc.inc(nc.vector.tensor_copy(wf_fm[:, t * 128:(t + 1) * 128], ps_wfT[:, :]))
        sems.setdefault("_gt_ev", {})[t] = vc.n
    wf_done = vc.n

    @E_.tensor
    def _(te):
        te.wait_ge(vc.h, wf_done)
        for ch in range(2):
            ins = nc.tensor.matmul(ps_wex[:, ch * 512:(ch + 1) * 512], ex8[:, :],
                                   wf_fm[:, ch * 512:(ch + 1) * 512], start=True, stop=True)
        tn.inc(ins)
    wex_ok = tn.n

    @E_.vector
    def _(v):
        v.wait_ge(tn.h, wex_ok)
        vc.inc(nc.vector.tensor_tensor(sb["low_w"][:, :], low_sb[:, :], ps_wex[:, :],
                                       op=Op.mult))
    return vc.n


def emit_hash(nc, E_, sems, sb, src_fm, dst, hi, off):
    tn, sc, vc = sems["tns"], sems["sc"], sems["vc"]
    ps_a, ps_b2 = sb["ps_sc0"], sb["ps_sc1"]
    h1 = sb["h1_sb"]
    tf = sb["tf32"]
    hp1, hp2, hb1, hb2 = sb["hp1_sb"], sb["hp2_sb"], sb["hb1_sb"], sb["hb2_sb"]
    nhb1 = sb["nhb1"]
    HS = {}
    _H = sems.setdefault("_h", {})
    for ch in range(8):
        c0, c1 = ch * 512, (ch + 1) * 512
        sl = slice(off + c0, off + c1)
        pp = [ps_a, ps_b2][ch % 2]
        tfh = tf[:, (ch % 2) * 512:(ch % 2 + 1) * 512]

        @E_.tensor
        def _(te, ch=ch, sl=sl, pp=pp):
            te.wait_ge(vc.h, _H.get(("s1v", ch % 2), 0))
            te.wait_ge(vc.h, _H.get(("s2", ch % 2), 0))
            te.wait_ge(sc.h, sems.get("_a0", {}).get(("sc", ch % 2), 0))
            tn.inc(nc.tensor.matmul(pp[:, 0:512], hp1[:, hi * 128:(hi + 1) * 128],
                                    src_fm[:, sl], start=True, stop=True))
        m1_ok = tn.n

        @E_.scalar
        def _(s, ch=ch, m1_ok=m1_ok, pp=pp):
            s.wait_ge(tn.h, m1_ok)
            s.wait_ge(vc.h, _H.get(("tfw", ch % 2), 0))
            sc.inc(nc.scalar.activation(tf[:, (ch % 2) * 512:(ch % 2 + 1) * 512],
                                        pp[:, 0:512], AF.Exp, scale=-1.0,
                                        bias=nhb1[:, hi:hi + 1]))
        e_ok = sc.n

        @E_.vector
        def _(v, ch=ch, c0=c0, c1=c1, sl=sl, pp=pp, e_ok=e_ok):
            v.wait_ge(sc.h, e_ok)
            if ch == 0:
                v.wait_ge(sc.h, sems.get("_a0", {}).get(("hashh1",), 0))
            tfh = tf[:, (ch % 2) * 512:(ch % 2 + 1) * 512]
            vc.inc(nc.vector.tensor_scalar(tfh, tfh, 1.0, None, op0=Op.add))
            v.wait_ge(vc.h, vc.n)
            vc.inc(nc.vector.reciprocal(tfh, tfh))
            v.wait_ge(vc.h, vc.n)
            vc.inc(nc.vector.tensor_scalar(h1[:, c0:c1], pp[:, 0:512],
                                           hb1[:, hi:hi + 1], None, op0=Op.add))
            _H[("s1v", ch % 2)] = vc.n
            v.wait_ge(vc.h, vc.n)
            vc.inc(nc.vector.tensor_tensor(h1[:, c0:c1], h1[:, c0:c1], tfh, op=Op.mult))
            _H[("tfw", ch % 2)] = vc.n
            v.wait_ge(vc.h, vc.n)
            vc.inc(nc.vector.tensor_tensor(h1[:, c0:c1], h1[:, c0:c1], src_fm[:, sl],
                                           op=Op.add))
    h1_all = vc.n

    for ch in range(8):
        c0, c1 = ch * 512, (ch + 1) * 512
        pp = [ps_a, ps_b2][ch % 2]

        @E_.tensor
        def _(te, ch=ch, c0=c0, c1=c1, pp=pp, h1_all=h1_all):
            te.wait_ge(vc.h, h1_all)
            te.wait_ge(vc.h, _H.get(("s2", ch % 2), 0))
            tn.inc(nc.tensor.matmul(pp[:, 0:512], hp2[:, hi * 128:(hi + 1) * 128],
                                    h1[:, c0:c1], start=True, stop=True))
        m2_ok = tn.n

        @E_.vector
        def _(v, ch=ch, c0=c0, c1=c1, pp=pp, m2_ok=m2_ok):
            v.wait_ge(tn.h, m2_ok)
            v.wait_ge(sc.h, _H.get(("sgp", ch % 2), 0))
            vc.inc(nc.vector.tensor_tensor(tf[:, (ch % 2) * 512:(ch % 2 + 1) * 512],
                                           pp[:, 0:512], h1[:, c0:c1], op=Op.add))
            _H[("s2", ch % 2)] = vc.n
        t2_ok = vc.n

        @E_.scalar
        def _(s, ch=ch, c0=c0, c1=c1, t2_ok=t2_ok):
            s.wait_ge(vc.h, t2_ok)
            sc.inc(nc.scalar.activation(dst[:, c0:c1],
                                        tf[:, (ch % 2) * 512:(ch % 2 + 1) * 512],
                                        AF.Sign, bias=hb2[:, hi:hi + 1]))
            _H[("sgp", ch % 2)] = sc.n
    return sc.n


PROBE_BITS = [128, 64, 32, 16, 8, 4, 2, 1]


def emit_strip_l0(nc, E_, sems, sb, ctx, i):
    tn, sc, vc = sems["tns"], sems["sc"], sems["vc"]
    b = ctx["b"]
    qr_p, qh_p, kh_p, oslab = ctx["qr_p"], ctx["qh_p"], ctx["kh_p"], ctx["oslab"]
    pb, negc, pred, cstar, rvec, cnt = (ctx["pb"], ctx["negc"], ctx["pred"],
                                        ctx["cstar"], ctx["rvec"], ctx["cnt"])
    svec = ctx["svec"]
    ps_av = ctx["ps_av"]
    krT, vtm = sb["krT"], sb["vtm"]
    ps_sim = sb["ps_sim0"]
    ps_sc = [sb["ps_sc0"], sb["ps_sc1"]]
    ps_tr4 = sb["ps_tr4"]
    sim_sb2, E_sb2 = sb["sim_sb"], sb["E_sb"]
    prb_scr = sb["prb_scr"]
    kg_sb, ieq_sb, kt_sb = sb["kg_sb"], sb["ieq_sb"], sb["kt_sb"]
    p_sb = sb["p_sb"]
    attn_sb = sb["attn_sb"]
    atT_ev = sb["atT_ev"]
    onrm, dvec = sb["onrm"], sb["dvec"]
    zer1, triu, triu01 = sb["zer1"], sb["triu_sb"], sb["triu01"]
    A0 = sems.setdefault("_a0", {})
    sctr = A0.get("sctr", 0)
    A0["sctr"] = sctr + 1
    par = sctr % 4

    L = 128 * (i + 1)
    Q0 = 128 * i
    sim_sb = sim_sb2[:, par * 4096:par * 4096 + 4096]
    E_sb = E_sb2[:, par * 4096:par * 4096 + 4096]
    topk = L > NR
    if topk:
        for cc in range(ceil_div(L, 512)):
            w = min(512, L - cc * 512)
            psim = [sb["ps_sim0"], sb["ps_sim1"]][cc % 2]

            @E_.tensor
            def _(te, cc=cc, w=w, psim=psim):
                te.wait_ge(vc.h, A0.get(("sime", cc % 2), 0))
                tn.inc(nc.tensor.matmul(psim[:, 0:w], qh_p[:, Q0:Q0 + 128],
                                        kh_p[:, cc * 512:cc * 512 + w],
                                        start=True, stop=True))
            mm_ok = tn.n

            @E_.vector
            def _(v, cc=cc, w=w, mm_ok=mm_ok, sim_sb=sim_sb, psim=psim):
                v.wait_ge(tn.h, mm_ok)
                v.wait_ge(sc.h, A0.get(("prbsc", par), 0))
                vc.inc(nc.vector.tensor_copy(sim_sb[:, cc * 512:cc * 512 + w],
                                             psim[:, 0:w]))
                A0[("sime", cc % 2)] = vc.n
                if (cc + 1) * 512 >= L:
                    v.wait_ge(vc.h, vc.n)
                    vc.inc(nc.vector.tensor_tensor(sim_sb[:, Q0:Q0 + 128],
                                                   sim_sb[:, Q0:Q0 + 128],
                                                   triu[:, :], op=Op.add))
    sim_ok = vc.n

    for ch in range(ceil_div(L, 512)):
        w = min(512, L - ch * 512)
        pp = ch % 2

        @E_.tensor
        def _(te, ch=ch, w=w, pp=pp):
            te.wait_ge(sc.h, A0.get(("sc", pp), 0))
            te.wait_ge(vc.h, A0.get(("scv",), 0))
            t0 = ch * 512
            tn.inc(nc.tensor.matmul(
                ps_sc[pp][:, 0:w],
                qr_p[:, Q0:Q0 + 128],
                krT[:, b * 4096 + t0:b * 4096 + t0 + w],
                start=True, stop=True))
        s_ok = tn.n

        @E_.scalar
        def _(s, ch=ch, w=w, pp=pp, s_ok=s_ok, E_sb=E_sb):
            s.wait_ge(tn.h, s_ok)
            s.wait_ge(vc.h, A0.get(("attnmul", par), 0))
            sc.inc(nc.scalar.activation(E_sb[:, ch * 512:ch * 512 + w],
                                        ps_sc[pp][:, 0:w], AF.Exp,
                                        scale=0.08838834764831845))
            A0[("sc", pp)] = sc.n
    exp_ok = sc.n
    yield

    gp = sems["gp"]
    if topk:
        @E_.gpsimd
        def _(g, sim_ok=sim_ok):
            g.wait_ge(vc.h, sim_ok)
            gp.inc(g.memset(negc[:, :], 129.0))
            gp.inc(g.memset(pb[:, :], float(129 - 2 * PROBE_BITS[0])))
            g.wait_ge(gp.h, gp.n)
        pb_ok = gp.n
    yield
    for k, bit in enumerate(PROBE_BITS):
        if topk:
            @E_.scalar
            def _(s_, k=k, L=L, sim_sb=sim_sb, pb_ok=pb_ok):
                s_.wait_ge(gp.h, pb_ok)
                sc.inc(nc.scalar.activation(prb_scr[:, 0:L], sim_sb[:, 0:L],
                                            AF.Sign, bias=pb[:, 0:1],
                                            accum_out=svec[:, k:k + 1]))
            prb_ok = sc.n

            @E_.gpsimd
            def _(g, k=k, bit=bit, L=L, prb_ok=prb_ok):
                g.wait_ge(sc.h, prb_ok)
                gp.inc(g.tensor_scalar(pred[:, :], svec[:, k:k + 1],
                                       float(2 * NR - L), float(-2 * bit),
                                       op0=Op.is_ge, op1=Op.mult))
                g.wait_ge(gp.h, gp.n)
                gp.inc(g.tensor_tensor(negc[:, :], negc[:, :], pred[:, :],
                                       op=Op.add))
                g.wait_ge(gp.h, gp.n)
                if k < len(PROBE_BITS) - 1:
                    gp.inc(g.tensor_scalar(pb[:, :], negc[:, :],
                                           float(-2 * PROBE_BITS[k + 1]),
                                           None, op0=Op.add))
                else:
                    g.wait_ge(vc.h, A0.get(("attnmulctx", ctx["id"]), 0))
                    gp.inc(g.tensor_scalar(cstar[:, :], negc[:, :],
                                           -1.0, 1.0, op0=Op.mult, op1=Op.add))
                g.wait_ge(gp.h, gp.n)
            pb_ok = gp.n
        yield
    if topk:
        A0[("prbsc", par)] = sc.n
        cstar_ok = gp.n

        @E_.vector
        def _(v, exp_ok=exp_ok, L=L, sim_sb=sim_sb, E_sb=E_sb, cstar_ok=cstar_ok):
            v.wait_ge(tn.h, A0.get(("trdone",), 0))
            v.wait_ge(gp.h, cstar_ok)
            vc.inc(nc.vector.tensor_scalar(kg_sb[:, 0:L], sim_sb[:, 0:L],
                                           cstar[:, 0:1], None, op0=Op.is_gt,
                                           op1=Op.add, accum_out=cnt[:, :]))
            v.wait_ge(vc.h, vc.n)
            vc.inc(nc.vector.tensor_scalar(rvec[:, :], cnt[:, :], -1.0, float(NR),
                                           op0=Op.mult, op1=Op.add))
            vc.inc(nc.vector.tensor_scalar(ieq_sb[:, 0:L], sim_sb[:, 0:L],
                                           cstar[:, 0:1], None, op0=Op.is_equal))
            v.wait_ge(vc.h, vc.n)
            vc.inc(nc.vector.tensor_tensor_scan(p_sb[:, 0:L], ieq_sb[:, 0:L],
                                                zer1[:, 0:1].to_broadcast([128, L]),
                                                0.0, op0=Op.add, op1=Op.add))
            v.wait_ge(vc.h, vc.n)
            vc.inc(nc.vector.scalar_tensor_tensor(kt_sb[:, 0:L], p_sb[:, 0:L],
                                                  rvec[:, 0:1], ieq_sb[:, 0:L],
                                                  op0=Op.is_le, op1=Op.mult))
            v.wait_ge(vc.h, vc.n)
            vc.inc(nc.vector.tensor_tensor(kg_sb[:, 0:L], kg_sb[:, 0:L],
                                           kt_sb[:, 0:L], op=Op.add))
            v.wait_ge(sc.h, exp_ok)
            v.wait_ge(vc.h, vc.n)
            vc.inc(nc.vector.tensor_tensor(attn_sb[:, 0:L], kg_sb[:, 0:L],
                                           E_sb[:, 0:L], op=Op.mult))
            A0[("attnmul", par)] = vc.n
            A0[("attnmulctx", ctx["id"])] = vc.n
    else:
        @E_.vector
        def _(v, exp_ok=exp_ok, L=L, E_sb=E_sb):
            v.wait_ge(sc.h, exp_ok)
            v.wait_ge(tn.h, A0.get(("trdone",), 0))
            vc.inc(nc.vector.tensor_copy(attn_sb[:, 0:L], E_sb[:, 0:L]))
            v.wait_ge(vc.h, vc.n)
            vc.inc(nc.vector.tensor_tensor(attn_sb[:, Q0:Q0 + 128],
                                           attn_sb[:, Q0:Q0 + 128],
                                           triu01[:, :], op=Op.mult))
            A0[("attnmul", par)] = vc.n
            A0[("attnmulctx", ctx["id"])] = vc.n
    attn_ok = vc.n

    ntt = L // 128
    for g4 in range(ceil_div(ntt, 4)):
        j0 = g4 * 4
        jn = min(4, ntt - j0)
        pp = g4 % 2

        @E_.tensor
        def _(te, j0=j0, jn=jn, attn_ok=attn_ok):
            te.wait_ge(vc.h, attn_ok)
            te.wait_ge(vc.h, A0.get(("tre",), 0))
            for j in range(jn):
                ins = nc.tensor.transpose(ps_tr4[:, j * 128:(j + 1) * 128],
                                          attn_sb[:, (j0 + j) * 128:(j0 + j + 1) * 128],
                                          sb["idm"][:, :])
            tn.inc(ins)
            A0[("trdone",)] = tn.n
        tr_ok = tn.n

        @E_.vector
        def _(v, jn=jn, pp=pp, tr_ok=tr_ok):
            v.wait_ge(tn.h, tr_ok)
            v.wait_ge(tn.h, A0.get(("av", pp), 0))
            vc.inc(nc.vector.tensor_copy(atT_ev[:, pp * 512:pp * 512 + jn * 128],
                                         ps_tr4[:, 0:jn * 128]))
            A0[("tre",)] = vc.n
        ev_ok = vc.n

        @E_.tensor
        def _(te, g4=g4, j0=j0, jn=jn, pp=pp, ev_ok=ev_ok):
            te.wait_ge(vc.h, ev_ok)
            if g4 == 0:
                te.wait_ge(vc.h, ctx["aveq"][0])
            for j in range(jn):
                tb = j0 + j
                ins = nc.tensor.matmul(
                    ps_av[:, 0:129],
                    atT_ev[:, pp * 512 + j * 128:pp * 512 + (j + 1) * 128],
                    vtm[:, (b * NT + tb) * 132:(b * NT + tb) * 132 + 129],
                    start=(tb == 0), stop=(tb == ntt - 1))
            tn.inc(ins)
            A0[("av", pp)] = tn.n
    av_ok = tn.n

    @E_.vector
    def _(v, av_ok=av_ok):
        v.wait_ge(tn.h, av_ok)
        v.wait_ge(tn.h, A0.get(("otr",), 0))
        vc.inc(nc.vector.tensor_copy(dvec[:, :], ps_av[:, 128:129]))
        v.wait_ge(vc.h, vc.n)
        vc.inc(nc.vector.reciprocal(dvec[:, :], dvec[:, :]))
        v.wait_ge(vc.h, vc.n)
        vc.inc(nc.vector.tensor_scalar(onrm[:, :], ps_av[:, 0:128], dvec[:, 0:1],
                                       None, op0=Op.mult))
        ctx["aveq"][0] = vc.n
    nrm_ok = vc.n

    @E_.tensor
    def _(te, nrm_ok=nrm_ok):
        te.wait_ge(vc.h, nrm_ok)
        te.wait_ge(vc.h, A0.get(("tre",), 0))
        tn.inc(nc.tensor.transpose(ps_tr4[:, 0:128], onrm[:, :], sb["idm"][:, :]))
        A0[("trdone",)] = tn.n
        A0[("otr",)] = tn.n
    otr_ok = tn.n

    ring = ctx.get("ring", False)
    ocol = ((i % 16) if ring else i) * 128

    @E_.vector
    def _(v, otr_ok=otr_ok, i=i, ocol=ocol, ring=ring):
        v.wait_ge(tn.h, otr_ok)
        if ring and i % 8 == 0 and i >= 16:
            v.wait_ge(sems["ds"].h, ctx["qtrdma"][(i // 8) % 2])
        vc.inc(nc.vector.tensor_copy(oslab[:, ocol:ocol + 128],
                                     ps_tr4[:, 0:128]))
        A0[("tre",)] = vc.n
    ctx["done"] = vc.n


def build_p0(core_inputs):
    nc = bass.Bass("TRN2", target_bir_lowering=False)
    xT = nc.dram_tensor("xT", [E, TOK], F32, kind="ExternalInput").ap()
    o_xn = nc.dram_tensor("o_xn", [E, TOK], BF16, kind="ExternalOutput").ap()
    with (
        nc.sbuf_tensor("x_sb", [128, NE * TOK], BF16) as x_sb,
        nc.sbuf_tensor("sq", [128, TOK], BF16) as sq,
        nc.sbuf_tensor("ones_col", [128, 1], BF16) as ones_col,
        nc.sbuf_tensor("ones_row", [1, 128], BF16) as ones_row,
        nc.sbuf_tensor("mrow", [1, 2 * TOK], F32) as mrow,
        nc.sbuf_tensor("eps", [1, 1], F32) as eps,
        nc.sbuf_tensor("mrow_bf", [1, 2 * TOK], BF16) as mrow_bf,
        nc.sbuf_tensor("mB", [128, 2 * TOK], BF16) as mB,
        nc.sbuf_tensor("xn_t", [128, 2 * TOK], BF16) as xn_t,
        nc.psum_tensor("ps_stat", [1, 2 * TOK], F32) as ps_stat,
        nc.psum_tensor("ps_bcast", [128, 2 * TOK], F32) as ps_b,
        nc.semaphore("dsem") as dh,
        nc.semaphore("csem") as ch,
        nc.semaphore("tsem") as th,
        nc.semaphore("vsem") as vh,
        nc.semaphore("gsem") as gh,
        nc.Block() as block,
    ):
        Em = Emitter(block, nc)
        sems = {"dma": Sem(dh), "sc": Sem(ch), "vc": Sem(vh), "tns": Sem(th),
                "gp": Sem(gh)}
        sb = dict(ps_stat=ps_stat, sq=sq, ones_col=ones_col, ones_row=ones_row,
                  mrow=mrow, mrow_bf=mrow_bf, ps_bcast=ps_b, mB=mB, xn_t=xn_t,
                  eps=eps)
        d = sems["dma"]

        @Em.gpsimd
        def _(g):
            g.memset(ones_col[:, :], 1.0)
            g.memset(ones_row[:, :], 1.0)
            sems["gp"].inc(g.memset(eps[:, :], 1e-5))
            xr = xT.rearrange("(t p) m -> t p m", p=128)
            for t in range(NE):
                d.inc(g.dma_start(out=x_sb[:, t * TOK:(t + 1) * TOK], in_=xr[t]), 16)
        loaded_d, loaded_c = d.n, sems["gp"].n

        def fwait(e):
            e.wait_ge(d.h, loaded_d)
            e.wait_ge(sems["gp"].h, loaded_c)
        Em.scalar(fwait)
        Em.tensor(fwait)
        Em.vector(fwait)

        emit_ln(nc, Em, sems, sb, x_sb, o_xn)
        final = d.n

        @Em.sync
        def _(sy):
            sy.wait_ge(d.h, final)
    return nc




def emit_barrier(nc, E_, sems):
    targets = [(k, sems[k].n) for k in ("dma", "ds", "dg", "tns", "sc", "vc", "gp", "cc")
               if k in sems and sems[k].n > 0]

    def w(e):
        for k, n in targets:
            e.wait_ge(sems[k].h, n)
    E_.sync(w)
    E_.gpsimd(w)
    E_.tensor(w)
    E_.scalar(w)
    E_.vector(w)


def emit_pairs_l0(nc, E_, sems, sb, qspill, oslab_sink, npairs=8):
    d, ds, tn, sc, vc = (sems["dma"], sems["ds"], sems["tns"], sems["sc"],
                         sems["vc"])
    A0 = sems.setdefault("_a0", {})
    smallv, svec = sb["smallv"], sb["svec"]
    try:
        ring = int(sb["oslab"].shape[1]) < NT * 128
    except Exception:
        ring = False
    for g in range(npairs // 2):
        ctxs = []
        for idx, p in enumerate((2 * g, 2 * g + 1)):
            b, hi = p // 4, p % 4
            sfx = "" if idx == 0 else "B"
            qr_p, qh_p, kh_p = sb["qr_p" + sfx], sb["qh_p" + sfx], sb["kh_p" + sfx]
            oslab = sb["oslab" + sfx]

            @E_.sync
            def _(sy, p=p, qr_p=qr_p):
                sy.wait_ge(tn.h, A0.get(("pairend",), 0))
                qsp = qspill.rearrange("(pair q) m -> pair q m", q=128)
                d.inc(sy.dma_start(out=qr_p[:, :], in_=qsp[p]), 16)
            qr_ok = d.n

            @E_.tensor
            def _(te, qr_ok=qr_ok):
                te.wait_ge(d.h, qr_ok)
                te.wait_ge(sc.h, A0.get(("sc", 0), 0))
                te.wait_ge(sc.h, A0.get(("sc", 1), 0))
            emit_hash(nc, E_, sems, sb, qr_p, qh_p, hi, 0)
            emit_hash(nc, E_, sems, sb, sb["krT"], kh_p, hi, b * 4096)
            o = idx * 8
            ctxs.append(dict(b=b, hi=hi, id=idx, qr_p=qr_p, qh_p=qh_p, kh_p=kh_p,
                             oslab=oslab, pb=smallv[:, o:o + 1],
                             negc=smallv[:, o + 1:o + 2], pred=smallv[:, o + 2:o + 3],
                             cstar=smallv[:, o + 3:o + 4], rvec=smallv[:, o + 4:o + 5],
                             cnt=smallv[:, o + 5:o + 6],
                             svec=svec[:, idx * 8:(idx + 1) * 8],
                             ps_av=sb["ps_av0" if idx == 0 else "ps_av1"],
                             aveq=[0], done=0, ring=ring,
                             qtrdma=[0, 0]))
        A0[("sc", 0)] = sc.n
        A0[("sc", 1)] = sc.n
        A0[("scv",)] = vc.n
        def mk(cidx, j):
            g_ = emit_strip_l0(nc, E_, sems, sb, ctxs[cidx], j)
            next(g_)
            return g_
        nxt = {0: mk(0, 0), 1: mk(1, 0)}
        for j in range(NT):
            gA, gB = nxt.pop(0), nxt.pop(1)
            next(gA); next(gB)
            for _k in range(len(PROBE_BITS)):
                next(gA); next(gB)
            if j + 1 < NT:
                nxt[0] = mk(0, j + 1)
                nxt[1] = mk(1, j + 1)
            for _ in gA:
                pass
            for _ in gB:
                pass
            if j % 8 == 7:
                q = j // 8
                for idx, p in enumerate((2 * g, 2 * g + 1)):
                    osl = ctxs[idx]["oslab"]
                    s0 = (q % 2) * TOK if ring else q * TOK
                    oslab_sink(p, osl[:, s0:s0 + TOK], q, ctxs[idx]["done"])
                    ctxs[idx]["qtrdma"][q % 2] = sems["ds"].n
        A0[("hashh1",)] = sems["sc"].n
        fin = sems["ds"].n

        @E_.vector
        def _(v_, fin=fin):
            v_.wait_ge(sems["ds"].h, fin)
        A0[("pairend",)] = tn.n
    return d.n


def build_l0attn(cs, npairs=1):
    nc = bass.Bass("TRN2", target_bir_lowering=False)
    g_xn = nc.dram_tensor("g_xn", [NCORE * E, TOK], BF16, kind="ExternalInput").ap()
    w_qT = nc.dram_tensor("w_qT", [E, 512], F32, kind="ExternalInput").ap()
    w_kT = nc.dram_tensor("w_kT", [E, 128], F32, kind="ExternalInput").ap()
    w_vT = nc.dram_tensor("w_vT", [E, 128], F32, kind="ExternalInput").ap()
    hp1 = nc.dram_tensor("hp1", [512, 128], F32, kind="ExternalInput").ap()
    hp2 = nc.dram_tensor("hp2", [512, 128], F32, kind="ExternalInput").ap()
    hb1 = nc.dram_tensor("hb1", [128, 4], F32, kind="ExternalInput").ap()
    hb2 = nc.dram_tensor("hb2", [128, 4], F32, kind="ExternalInput").ap()
    cossd = nc.dram_tensor("cossd", [S, D], BF16, kind="ExternalInput").ap()
    sinsd = nc.dram_tensor("sinsd", [S, D], BF16, kind="ExternalInput").ap()
    idm_d = nc.dram_tensor("idm", [128, 128], BF16, kind="ExternalInput").ap()
    triu_d = nc.dram_tensor("triu", [128, 128], BF16, kind="ExternalInput").ap()
    triu01_d = nc.dram_tensor("triu01", [128, 128], BF16, kind="ExternalInput").ap()
    qspill = nc.dram_tensor("qspill", [8 * 128, S], BF16)
    o_qr = nc.dram_tensor("o_qr", [128, S], BF16, kind="ExternalOutput").ap()
    o_qh = nc.dram_tensor("o_qh", [128, S], BF16, kind="ExternalOutput").ap()
    o_kh = nc.dram_tensor("o_kh", [128, S], BF16, kind="ExternalOutput").ap()
    o_out = nc.dram_tensor("o_out", [npairs * 128, S], BF16, kind="ExternalOutput").ap()
    o_qh2 = nc.dram_tensor("o_qh2", [128, S], BF16, kind="ExternalOutput").ap()

    with (
        nc.sbuf_tensor("krT", [128, 2 * S], BF16) as krT,
        nc.sbuf_tensor("vtm", [128, 2 * NT * 132], BF16) as vtm,
        nc.sbuf_tensor("idm_sb", [128, 128], BF16) as idm,
        nc.sbuf_tensor("triu_sb", [128, 128], BF16) as triu_sb,
        nc.sbuf_tensor("triu01_sb", [128, 128], BF16) as triu01_sb,
        nc.sbuf_tensor("nhb1_sb", [128, 4], F32) as nhb1_sb,
        nc.sbuf_tensor("hp1_sb", [128, 512], BF16) as hp1_sb,
        nc.sbuf_tensor("hp2_sb", [128, 512], BF16) as hp2_sb,
        nc.sbuf_tensor("hb1_sb", [128, 4], F32) as hb1_sb,
        nc.sbuf_tensor("hb2_sb", [128, 4], F32) as hb2_sb,
        nc.semaphore("dsem") as dh,
        nc.semaphore("dssem") as dsh,
        nc.semaphore("dgsem") as dgh,
        nc.semaphore("csem") as ch,
        nc.semaphore("tsem") as th,
        nc.semaphore("vsem") as vh,
        nc.semaphore("gsem") as gh,
        nc.Block() as block,
    ):
        Em = Emitter(block, nc)
        sems = {"dma": Sem(dh), "ds": Sem(dsh), "dg": Sem(dgh), "sc": Sem(ch),
                "vc": Sem(vh), "tns": Sem(th), "gp": Sem(gh)}
        sb = dict(krT=krT, vtm=vtm, idm=idm, triu_sb=triu_sb, triu01=triu01_sb,
                  nhb1=nhb1_sb, hp1_sb=hp1_sb, hp2_sb=hp2_sb, hb1_sb=hb1_sb,
                  hb2_sb=hb2_sb)
        d, dg = sems["dma"], sems["dg"]

        @Em.gpsimd
        def _(g):
            g.memset(vtm[:, :], 1.0)
            dg.inc(g.dma_start(out=hp1_sb[:, :].rearrange("p (h e) -> p h e", e=128),
                               in_=hp1.rearrange("(h p) e -> p h e", p=128)), 16)
            dg.inc(g.dma_start(out=hp2_sb[:, :].rearrange("p (h e) -> p h e", e=128),
                               in_=hp2.rearrange("(h p) e -> p h e", p=128)), 16)
            dg.inc(g.dma_start(out=hb1_sb[:, :], in_=hb1), 16)
            dg.inc(g.dma_start(out=hb2_sb[:, :], in_=hb2), 16)

        @Em.sync
        def _(sy):
            d.inc(sy.dma_start(out=idm[:, :], in_=idm_d), 16)
            d.inc(sy.dma_start(out=triu_sb[:, :], in_=triu_d), 16)
            d.inc(sy.dma_start(out=triu01_sb[:, :], in_=triu01_d), 16)

        with ExitStack() as stA:
            en = stA.enter_context
            specA = [("w_q_sb", [128, NE * 512], BF16), ("w_k_sb", [128, NE * 128], BF16),
                     ("w_v_sb", [128, NE * 128], BF16), ("xng", [128, 2 * 4096], BF16),
                     ("cossd_sb", [128, NT * 128], BF16), ("sinsd_sb", [128, NT * 128], BF16),
                     ("qr_tm", [128, 512], BF16), ("kr_tm", [128, 128], BF16),
                     ("tr_ev", [128, 512], BF16), ("rtmp", [128, 128], BF16)]
            for nm, shp, dt in specA:
                sb[nm] = en(nc.sbuf_tensor(nm, shp, dt))
            for nm, shp, dt in [("ps_q", [128, 512], F32), ("ps_k", [128, 128], F32),
                                ("ps_v", [128, 128], F32), ("ps_tr", [128, 512], BF16),
                                ("ps_trk", [128, 128], BF16)]:
                sb[nm] = en(nc.psum_tensor(nm, shp, dt))
            w_q_sb, w_k_sb, w_v_sb = sb["w_q_sb"], sb["w_k_sb"], sb["w_v_sb"]
            cossd_sb, sinsd_sb = sb["cossd_sb"], sb["sinsd_sb"]
            sb["cossd"], sb["sinsd"] = cossd_sb, sinsd_sb

            @Em.gpsimd
            def _(g):
                wqr = w_qT.rearrange("(t p) m -> t p m", p=128)
                wkr = w_kT.rearrange("(t p) m -> t p m", p=128)
                wvr = w_vT.rearrange("(t p) m -> t p m", p=128)
                for t in range(NE):
                    dg.inc(g.dma_start(out=w_q_sb[:, t * 512:(t + 1) * 512], in_=wqr[t]), 16)
                    dg.inc(g.dma_start(out=w_k_sb[:, t * 128:(t + 1) * 128], in_=wkr[t]), 16)
                    dg.inc(g.dma_start(out=w_v_sb[:, t * 128:(t + 1) * 128], in_=wvr[t]), 16)

            @Em.sync
            def _(sy):
                csr = cossd.rearrange("(t p) m -> t p m", p=128)
                ssr = sinsd.rearrange("(t p) m -> t p m", p=128)
                for t in range(NT):
                    d.inc(sy.dma_start(out=cossd_sb[:, t * 128:(t + 1) * 128], in_=csr[t]), 16)
                    d.inc(sy.dma_start(out=sinsd_sb[:, t * 128:(t + 1) * 128], in_=ssr[t]), 16)
            pre_d, pre_dg, pre_gp = d.n, dg.n, sems["gp"].n

            def fw(e):
                e.wait_ge(d.h, pre_d)
                e.wait_ge(dg.h, pre_dg)
            Em.tensor(fw)
            Em.vector(fw)
            emit_proj0(nc, Em, sems, sb, g_xn, qspill)
            emit_barrier(nc, Em, sems)

        with ExitStack() as stB:
            en = stB.enter_context
            for nm, shp, dt in [("qr_p", [128, S], BF16), ("qh_p", [128, S], BF16),
                                ("kh_p", [128, S], BF16), ("qr_pB", [128, S], BF16),
                                ("qh_pB", [128, S], BF16), ("kh_pB", [128, S], BF16),
                                ("oslabB", [128, S], BF16), ("h1_sb", [128, S], BF16),
                                ("tf32", [128, 1024], F32), ("sim_sb", [128, 4 * S], BF16),
                                ("E_sb", [128, 4 * S], BF16), ("kg_sb", [128, S], BF16),
                                ("ieq_sb", [128, S], FP16), ("kt_sb", [128, S], BF16),
                                ("p_sb", [128, S], FP16), ("atT_ev", [128, 2 * 512], BF16),
                                ("onrm", [128, 128], BF16), ("oslab", [128, S], BF16),
                                ("svec", [128, 16], F32), ("smallv", [128, 16], F32),
                                ("zer1", [128, 1], FP16)]:
                sb[nm] = en(nc.sbuf_tensor(nm, shp, dt))
            for nm, shp, dt in [("ps_sim0", [128, 512], F32),
                                ("ps_sim1", [128, 512], F32),
                                ("ps_sc0", [128, 512], F32), ("ps_sc1", [128, 512], F32),
                                ("ps_tr4", [128, 512], BF16), ("ps_av0", [128, 132], F32),
                                ("ps_av1", [128, 132], F32)]:
                sb[nm] = en(nc.psum_tensor(nm, shp, dt))
            smallv = sb["smallv"]
            qr_p, qh_p, kh_p, oslab = sb["qr_p"], sb["qh_p"], sb["kh_p"], sb["oslab"]
            sb.update(attn_sb=sb["kg_sb"], prb_scr=sb["h1_sb"],
                      pb=smallv[:, 0:1], negc=smallv[:, 1:2], pred=smallv[:, 2:3],
                      cstar=smallv[:, 3:4], rvec=smallv[:, 4:5], cnt=smallv[:, 5:6],
                      dvec=smallv[:, 6:7])
            zer1 = sb["zer1"]

            sb["early_tap"] = o_qh2

            @Em.vector
            def _(v):
                sems["vc"].inc(nc.vector.memset(zer1[:, :], 0.0))
                sems["vc"].inc(nc.vector.tensor_scalar(nhb1_sb[:, :], hb1_sb[:, :],
                                                       -1.0, None, op0=Op.mult))
            z_ok = sems["vc"].n

            @Em.scalar
            def _(s_):
                s_.wait_ge(sems["vc"].h, z_ok)

            def sink(p, oslab_sl, qtr, done):
                @Em.sync
                def _(sy, p=p, qtr=qtr, so=done):
                    sy.wait_ge(sems["vc"].h, so)
                    if p == 0 and qtr == 3:
                        sems["ds"].inc(sy.dma_start(out=o_qr, in_=qr_p[:, :]), 16)
                        sems["ds"].inc(sy.dma_start(out=o_qh, in_=qh_p[:, :]), 16)
                        sems["ds"].inc(sy.dma_start(out=o_kh, in_=kh_p[:, :]), 16)
                    if p < (o_out.shape[0] // 128):
                        sems["ds"].inc(sy.dma_start(
                            out=o_out[p * 128:(p + 1) * 128, qtr * TOK:(qtr + 1) * TOK],
                            in_=oslab_sl), 16)
                fin = sems["ds"].n

                @Em.scalar
                def _(s_, fin=fin):
                    s_.wait_ge(sems["ds"].h, fin)

            emit_pairs_l0(nc, Em, sems, sb, qspill, sink, npairs=npairs)
            emit_barrier(nc, Em, sems)
    return nc


def emit_colslab_load(nc, E_, sems, g, dst_sb, src_hbm, eb, nblk, blkrows=128):
    dg = sems["dg"]
    src = src_hbm.rearrange("(j p) m -> p j m", p=blkrows)
    dg.inc(g.dma_start(out=dst_sb[:blkrows, :].rearrange("p (j m) -> p j m", m=128),
                       in_=src[:, :, eb * 128:(eb + 1) * 128]), 16)
    return dg.n


def emit_evac_l0(nc, E_, sems, sb, a2a_out, g_xn, xT_hbm, x1_hbm, core_id):
    d, ds, dg, tn, sc, vc = (sems["dma"], sems["ds"], sems["dg"], sems["tns"],
                             sems["sc"], sems["vc"])
    at_sb = sb["at_sb"]
    xn_sb = sb["xn_sb"]
    wo_sl = sb["wo_sl"]
    wb_sl = sb["wb_sl"]
    w_lB_sb = sb["w_lB_sb"]
    low_w = sb["low_w"]
    bb_sb = sb["bb_sb"]
    ps = [sb["ps_e0"], sb["ps_e1"]]
    xt_f = sb["xt_f"]
    ev_f = sb["ev_f"]
    x1_sb = sb["x1_sb"]
    xr = xT_hbm.rearrange("(t p) m -> t p m", p=128)
    for eb in range(NE):
        par = eb % 2

        @E_.gpsimd
        def _(g, eb=eb, par=par):
            if eb >= 2:
                g.wait_ge(tn.h, sems.get("_e0", {}).get(eb - 2, 0))
            emit_colslab_load(nc, E_, sems, g, wo_sl[:, par * 4096:(par + 1) * 4096],
                              sb["w_oT_h"], eb, NE)
            emit_colslab_load(nc, E_, sems, g, wb_sl[:, par * 4096:(par + 1) * 4096],
                              sb["w_bT_h"], eb, NE)
        w_ok = dg.n

        @E_.sync
        def _(sy, eb=eb, par=par):
            d.inc(sy.dma_start(out=xt_f[:, par * 1024:(par + 1) * 1024], in_=xr[eb]), 16)
        x_ok = d.n

        for ch in range(2):
            @E_.tensor
            def _(te, eb=eb, par=par, ch=ch, w_ok=w_ok):
                te.wait_ge(dg.h, w_ok)
                te.wait_ge(vc.h, sems.get("_e0s", {}).get((eb - 1, ch), 0))
                c0 = ch * 512
                wo = wo_sl[:, par * 4096:(par + 1) * 4096]
                wb = wb_sl[:, par * 4096:(par + 1) * 4096]
                for j in range(NE):
                    nc.tensor.matmul(ps[ch][:, :], wo[:, j * 128:(j + 1) * 128],
                                     at_sb[:, j * 1024 + c0:j * 1024 + c0 + 512],
                                     start=(j == 0), stop=False)
                for j in range(NE):
                    nc.tensor.matmul(ps[ch][:, :], wb[:, j * 128:(j + 1) * 128],
                                     xn_sb[:, j * 1024 + c0:j * 1024 + c0 + 512],
                                     start=False, stop=False)
                ins = nc.tensor.matmul(ps[ch][:, :], w_lB_sb[:, eb * 128:(eb + 1) * 128],
                                       low_w[:, c0:c0 + 512], start=False, stop=True)
                tn.inc(ins)
            mm_ok = tn.n
            sems.setdefault("_e0", {})[eb] = tn.n

            @E_.vector
            def _(v, eb=eb, ch=ch, mm_ok=mm_ok):
                v.wait_ge(tn.h, mm_ok)
                v.wait_ge(vc.h, sems.get("_e0v", {}).get((eb - 1, ch), 0))
                vc.inc(nc.vector.tensor_scalar(ev_f[:, ch * 512:(ch + 1) * 512],
                                               ps[ch][:, :], bb_sb[:, eb:eb + 1],
                                               None, op0=Op.add))
                sems.setdefault("_e0s", {})[(eb, ch)] = vc.n
        ev_ok = vc.n

        @E_.vector
        def _(v, eb=eb, par=par, ev_ok=ev_ok, x_ok=x_ok):
            v.wait_ge(vc.h, ev_ok)
            v.wait_ge(d.h, x_ok)
            v.wait_ge(ds.h, sems.get("_e0d", {}).get(eb - 2, 0))
            vc.inc(nc.vector.tensor_tensor(x1_sb[:, par * 1024:(par + 1) * 1024],
                                           ev_f[:, :], xt_f[:, par * 1024:(par + 1) * 1024],
                                           op=Op.add))
            sems.setdefault("_e0v", {})[(eb, 0)] = vc.n
            sems["_e0v"][(eb, 1)] = vc.n
        x1_ok = vc.n

        @E_.sync
        def _(sy, eb=eb, par=par, x1_ok=x1_ok):
            sy.wait_ge(vc.h, x1_ok)
            ds.inc(sy.dma_start(out=x1_hbm[eb * 128:(eb + 1) * 128, :],
                                in_=x1_sb[:, par * 1024:(par + 1) * 1024]), 16)
            sems.setdefault("_e0d", {})[eb] = ds.n
    return d.n


def emit_proj1(nc, E_, sems, sb, g_xn, q1spill, k1spill, v1spill):
    d, ds, dg, tn, sc, vc = (sems["dma"], sems["ds"], sems["dg"], sems["tns"],
                             sems["sc"], sems["vc"])
    xng = sb["xng"]
    wq, wk, wv = sb["w_q_sb"], sb["w_k_sb"], sb["w_v_sb"]
    ps_q, ps_k, ps_v = sb["ps_q"], sb["ps_k"], sb["ps_v"]
    ps_tr = sb["ps_tr"]
    qr_tm, kr_tm = sb["qr_tm"], sb["kr_tm"]
    tr_ev = sb["tr_ev"]
    rtmp = sb["rtmp"]
    cossd, sinsd = sb["cossd"], sb["sinsd"]
    v_ev = sb["v_ev"]
    for b in range(B):
        for tt in range(NT):
            it = b * NT + tt
            par = it % 2
            xbuf = xng[:, par * 4096:(par + 1) * 4096]

            @E_.sync
            def _(sy, b=b, tt=tt, it=it, xbuf=xbuf):
                if it >= 2:
                    sy.wait_ge(tn.h, sems.get("_p1x", {}).get(it - 2, 0))
                src = g_xn.rearrange("(r e p) m -> r p e m", e=NE, p=128)
                rr = 4 * b + tt // 8
                co = (tt % 8) * 128
                d.inc(sy.dma_start(out=xbuf.rearrange("p (e m) -> p e m", m=128),
                                   in_=src[rr, :, :, co:co + 128]), 16)
            xn_ok = d.n

            @E_.tensor
            def _(te, it=it, xbuf=xbuf, xn_ok=xn_ok):
                te.wait_ge(d.h, xn_ok)
                if it >= 1:
                    te.wait_ge(vc.h, sems.get("_p1e", {}).get(it - 1, 0))
                for e in range(NE):
                    st, sp = (e == 0), (e == NE - 1)
                    nc.tensor.matmul(ps_q[:, :], xbuf[:, e * 128:(e + 1) * 128],
                                     wq[:, e * 512:(e + 1) * 512], start=st, stop=sp)
                    nc.tensor.matmul(ps_k[:, :], xbuf[:, e * 128:(e + 1) * 128],
                                     wk[:, e * 512:(e + 1) * 512], start=st, stop=sp)
                    ins = nc.tensor.matmul(ps_v[:, :], xbuf[:, e * 128:(e + 1) * 128],
                                           wv[:, e * 512:(e + 1) * 512], start=st, stop=sp)
                tn.inc(ins)
            mm_ok = tn.n
            sems.setdefault("_p1x", {})[it] = tn.n

            @E_.vector
            def _(v, mm_ok=mm_ok):
                v.wait_ge(tn.h, mm_ok)
            cos_sl = cossd[:, tt * 128:(tt + 1) * 128]
            sin_sl = sinsd[:, tt * 128:(tt + 1) * 128]
            emit_rope(nc, E_, sems, ps_q, cos_sl, sin_sl, qr_tm, 4, rtmp)
            emit_rope(nc, E_, sems, ps_k, cos_sl, sin_sl, kr_tm, 4, rtmp)
            rope_ok = vc.n

            @E_.vector
            def _(v, it=it):
                v.wait_ge(ds.h, sems.get("_p1v", {}).get(it - 1, 0))
                vc.inc(nc.vector.tensor_copy(v_ev[:, :], ps_v[:, :]))
            vv_ok = vc.n
            sems.setdefault("_p1e", {})[it] = vc.n

            @E_.sync
            def _(sy, b=b, tt=tt, vv_ok=vv_ok):
                sy.wait_ge(vc.h, vv_ok)
                ds.inc(sy.dma_start(out=v1spill[b * S + tt * 128:b * S + (tt + 1) * 128, :],
                                    in_=v_ev[:, :]), 16)
                sems.setdefault("_p1v", {})[it] = ds.n

            for which, src_tm, spill in ((0, qr_tm, q1spill), (1, kr_tm, k1spill)):
                @E_.tensor
                def _(te, src_tm=src_tm, rope_ok=rope_ok, which=which):
                    te.wait_ge(vc.h, rope_ok)
                    te.wait_ge(vc.h, sems.get("_p1t", {}).get("last", 0))
                    for h in range(4):
                        ins = nc.tensor.transpose(ps_tr[:, h * 128:(h + 1) * 128],
                                                  src_tm[:, h * 128:(h + 1) * 128],
                                                  sb["idm"][:, :])
                    tn.inc(ins)
                tr_ok = tn.n

                @E_.vector
                def _(v, tr_ok=tr_ok):
                    v.wait_ge(tn.h, tr_ok)
                    v.wait_ge(ds.h, sems.get("_p1t", {}).get("spill", 0))
                    vc.inc(nc.vector.tensor_copy(tr_ev[:, :], ps_tr[:, :]))
                trev_ok = vc.n
                sems.setdefault("_p1t", {})["last"] = vc.n

                @E_.sync
                def _(sy, b=b, tt=tt, spill=spill, trev_ok=trev_ok):
                    sy.wait_ge(vc.h, trev_ok)
                    qsp = spill.rearrange("(pair q) m -> pair q m", q=128)
                    for h in range(4):
                        ds.inc(sy.dma_start(out=qsp[b * 4 + h, :, tt * 128:(tt + 1) * 128],
                                            in_=tr_ev[:, h * 128:(h + 1) * 128]), 16)
                sems.setdefault("_p1t", {})["spill"] = ds.n
    return d.n


def emit_ffn1_h(nc, E_, sems, sb, xn_own, h1_hbm, core_id=0):
    d, ds, dg, tn, sc, vc = (sems["dma"], sems["ds"], sems["dg"], sems["tns"],
                             sems["sc"], sems["vc"])
    xt32 = sb["xn_loc32"]
    V_sb = sb["V_sb"]
    s1_sb = sb["s1_sb"]
    ps = [sb["ps_e0"], sb["ps_e1"]]
    h1_ev = sb["h1_ev"]

    @E_.sync
    def _(sy):
        src = xn_own.rearrange("(e p) m -> e p m", p=128)
        for e in range(NE):
            d.inc(sy.dma_start(out=xt32[:, e * TOK:(e + 1) * TOK], in_=src[e]), 16)
    xall_ok = d.n
    for rb in range(8):
        @E_.tensor
        def _(te, rb=rb):
            te.wait_ge(d.h, xall_ok)
            if rb >= 1:
                te.wait_ge(vc.h, sems.get("_f1s", {}).get(rb - 1, 0))
            for e in range(NE):
                st, sp = (e == 0), (e == NE - 1)
                for ch in range(2):
                    ins = nc.tensor.matmul(ps[ch][:, :],
                                           V_sb[:, e * 1024 + rb * 128:e * 1024 + (rb + 1) * 128],
                                           xt32[:, e * TOK + ch * 512:e * TOK + (ch + 1) * 512],
                                           start=st, stop=sp)
            tn.inc(ins)
        mm_ok = tn.n

        @E_.vector
        def _(v, rb=rb, mm_ok=mm_ok):
            v.wait_ge(tn.h, mm_ok)
            v.wait_ge(ds.h, sems.get("_f1d", {}).get(rb - 1, 0))
            for ch in range(2):
                vc.inc(nc.vector.tensor_scalar(h1_ev[:, ch * 512:(ch + 1) * 512],
                                               ps[ch][:, :], s1_sb[:, rb:rb + 1],
                                               None, op0=Op.mult))
            sems.setdefault("_f1s", {})[rb] = vc.n
        h_ok = vc.n

        @E_.sync
        def _(sy, rb=rb, h_ok=h_ok):
            sy.wait_ge(vc.h, h_ok)
            ds.inc(sy.dma_start(out=h1_hbm[rb * 128:(rb + 1) * 128, :], in_=h1_ev[:, :]), 16)
            sems.setdefault("_f1d", {})[rb] = ds.n
    return d.n


def emit_attn1_pair(nc, E_, sems, sb, b, hi, oslab):
    tn, sc, vc = sems["tns"], sems["sc"], sems["vc"]
    q1, k1, vtm = sb["q1_p"], sb["k1_p"], sb["vtm"]
    ps_sc = [sb["ps_sc0"], sb["ps_sc1"]]
    ps_av = [sb["ps_av0"], sb["ps_av1"], sb["ps_av2"]]
    atT = sb["atT_ev"]
    ps_tr4 = sb["ps_tr4"]
    onrm, dvec = sb["onrm"], sb["dvec"]
    A1 = sems.setdefault("_a1", {})

    def avsl(j):
        t = ps_av[j // 3]
        o = (j % 3) * 132
        return t[:, o:o + 129], t[:, o + 128:o + 129]

    for qg in range(4):
        q0 = qg * 1024
        pend_av = None
        for tt in range(NT):
            pp = tt % 2
            ap4 = tt % 4

            @E_.tensor
            def _(te, qg=qg, tt=tt, pp=pp, q0=q0):
                te.wait_ge(sc.h, A1.get(("sc", pp), 0))
                if tt == 0:
                    te.wait_ge(vc.h, A1.get(("eq",), 0))
                for half in range(2):
                    ins = nc.tensor.matmul(
                        ps_sc[pp][:, half * 512:(half + 1) * 512],
                        k1[:, tt * 128:(tt + 1) * 128],
                        q1[:, q0 + half * 512:q0 + (half + 1) * 512],
                        start=True, stop=True)
                tn.inc(ins)
            s_ok = tn.n
            if pend_av is not None:
                pend_av()

            @E_.scalar
            def _(s_, tt=tt, pp=pp, ap4=ap4, s_ok=s_ok):
                s_.wait_ge(tn.h, s_ok)
                s_.wait_ge(tn.h, A1.get(("av", ap4), 0))
                sc.inc(nc.scalar.activation(atT[:, ap4 * 1024:(ap4 + 1) * 1024],
                                            ps_sc[pp][:, :], AF.Exp,
                                            scale=0.08838834764831845))
                A1[("sc", pp)] = sc.n
            e_ok = sc.n

            def pend_av(b=b, tt=tt, pp=pp, ap4=ap4, e_ok=e_ok):
                @E_.tensor
                def _(te):
                    te.wait_ge(sc.h, e_ok)
                    for j in range(8):
                        out, _dn = avsl(j)
                        ins = nc.tensor.matmul(
                            out,
                            atT[:, ap4 * 1024 + j * 128:ap4 * 1024 + (j + 1) * 128],
                            vtm[:, tt * 132:tt * 132 + 129],
                            start=(tt == 0), stop=(tt == NT - 1))
                    tn.inc(ins)
                    A1[("av", ap4)] = tn.n
        pend_av()
        av_ok = tn.n

        for j in range(8):
            out, dn = avsl(j)

            @E_.vector
            def _(v, j=j, av_ok=av_ok, out=out, dn=dn):
                v.wait_ge(tn.h, av_ok)
                vc.inc(nc.vector.tensor_copy(dvec[:, :], dn))
                v.wait_ge(vc.h, vc.n)
                vc.inc(nc.vector.reciprocal(dvec[:, :], dvec[:, :]))
                v.wait_ge(vc.h, vc.n)
                vc.inc(nc.vector.tensor_scalar(onrm[:, :], out[:, 0:128], dvec[:, 0:1],
                                               None, op0=Op.mult))
            n_ok = vc.n

            @E_.tensor
            def _(te, n_ok=n_ok, j=j):
                te.wait_ge(vc.h, n_ok)
                te.wait_ge(vc.h, A1.get(("tr",), 0))
                tn.inc(nc.tensor.transpose(ps_tr4[:, 0:128], onrm[:, :], sb["idm"][:, :]))
            t_ok = tn.n

            @E_.vector
            def _(v, t_ok=t_ok, qg=qg, j=j):
                v.wait_ge(tn.h, t_ok)
                vc.inc(nc.vector.tensor_copy(oslab[:, (qg * 8 + j) * 128:(qg * 8 + j + 1) * 128],
                                             ps_tr4[:, 0:128]))
                A1[("tr",)] = vc.n
        A1[("eq",)] = vc.n
    return vc.n


def emit_evac_l1(nc, E_, sems, sb, x1_hbm, o_x):
    d, ds, dg, tn, sc, vc = (sems["dma"], sems["ds"], sems["dg"], sems["tns"],
                             sems["sc"], sems["vc"])
    at_sb = sb["at_sb"]
    h1_sb2 = sb["h1_sb2"]
    wo_sl = sb["wo_sl"]
    wu_sl = sb["wu_sl"]
    ps = [sb["ps_e0"], sb["ps_e1"]]
    xt_f = sb["xt_f"]
    ev_f = sb["ev_f"]
    xr = x1_hbm.rearrange("(t p) m -> t p m", p=128)
    for eb in range(NE):
        par = eb % 2

        @E_.gpsimd
        def _(g, eb=eb, par=par):
            if eb >= 2:
                g.wait_ge(tn.h, sems.get("_e1", {}).get(eb - 2, 0))
            emit_colslab_load(nc, E_, sems, g, wo_sl[:, par * 4096:(par + 1) * 4096],
                              sb["w1_oT_h"], eb, NE)
            emit_colslab_load(nc, E_, sems, g, wu_sl[:, par * 1024:(par + 1) * 1024],
                              sb["w1_UT_h"], eb, 8)
        w_ok = dg.n

        @E_.sync
        def _(sy, eb=eb, par=par):
            sy.wait_ge(vc.h, sems.get("_e1v", {}).get(eb - 2, 0))
            d.inc(sy.dma_start(out=xt_f[:, par * 1024:(par + 1) * 1024], in_=xr[eb]), 16)
        x_ok = d.n

        for ch in range(2):
            @E_.tensor
            def _(te, eb=eb, par=par, ch=ch, w_ok=w_ok):
                te.wait_ge(dg.h, w_ok)
                te.wait_ge(vc.h, sems.get("_e1v", {}).get((eb - 1), 0))
                c0 = ch * 512
                wo = wo_sl[:, par * 4096:(par + 1) * 4096]
                wu = wu_sl[:, par * 1024:(par + 1) * 1024]
                for j in range(NE):
                    nc.tensor.matmul(ps[ch][:, :], wo[:, j * 128:(j + 1) * 128],
                                     at_sb[:, j * 1024 + c0:j * 1024 + c0 + 512],
                                     start=(j == 0), stop=False)
                for j in range(8):
                    ins = nc.tensor.matmul(ps[ch][:, :], wu[:, j * 128:(j + 1) * 128],
                                           h1_sb2[:, j * 1024 + c0:j * 1024 + c0 + 512],
                                           start=False, stop=(j == 7))
                tn.inc(ins)
            mm_ok = tn.n
            sems.setdefault("_e1", {})[eb] = tn.n

            @E_.vector
            def _(v, eb=eb, par=par, ch=ch, mm_ok=mm_ok, x_ok=x_ok):
                v.wait_ge(tn.h, mm_ok)
                v.wait_ge(d.h, x_ok)
                v.wait_ge(ds.h, sems.get("_e1d", {}).get(eb - 1, 0))
                c0 = ch * 512
                vc.inc(nc.vector.tensor_tensor(ev_f[:, c0:c0 + 512], ps[ch][:, :],
                                               xt_f[:, par * 1024 + c0:par * 1024 + c0 + 512],
                                               op=Op.add))
        ev_ok = vc.n
        sems.setdefault("_e1v", {})[eb] = vc.n

        @E_.sync
        def _(sy, eb=eb, ev_ok=ev_ok):
            sy.wait_ge(vc.h, ev_ok)
            ds.inc(sy.dma_start(out=o_x[eb * 128:(eb + 1) * 128, :], in_=ev_f[:, :]), 16)
            sems.setdefault("_e1d", {})[eb] = ds.n
    return d.n




def _scaffold(nc, st):
    names = [("dma", "dsem"), ("ds", "dssem"), ("dg", "dgsem"), ("sc", "csem"),
             ("tns", "tsem"), ("vc", "vsem"), ("gp", "gsem"), ("cc", "ccsem")]
    sems = {k: Sem(st.enter_context(nc.semaphore(n))) for k, n in names}
    block = st.enter_context(nc.Block())
    return sems, Emitter(block, nc)


_ALLOC_CTR = [0]


def _alloc(nc, st, sb, specs, psum=False):
    for nm, shp, dt in specs:
        _ALLOC_CTR[0] += 1
        uniq = f"{nm}_u{_ALLOC_CTR[0]}"
        sb[nm] = st.enter_context(
            nc.psum_tensor(uniq, shp, dt) if psum else nc.sbuf_tensor(uniq, shp, dt))


def _final_wait(E_, sems):
    fin_d, fin_dg, fin_ds = sems["dma"].n, sems["dg"].n, sems["ds"].n

    @E_.sync
    def _(sy):
        sy.wait_ge(sems["dma"].h, fin_d)
        if fin_dg:
            sy.wait_ge(sems["dg"].h, fin_dg)
        if fin_ds:
            sy.wait_ge(sems["ds"].h, fin_ds)


def _slab_loads(g, dg, pairs):
    for dst, src, m in pairs:
        srcr = src.rearrange("(t p) m -> t p m", p=128)
        ntt = srcr.shape[0]
        for t in range(ntt):
            dg.inc(g.dma_start(out=dst[:, t * m:(t + 1) * m], in_=srcr[t]), 16)


def build_ln(in_dtype):
    nc = bass.Bass("TRN2", target_bir_lowering=False)
    xT = nc.dram_tensor("xT", [E, TOK], in_dtype, kind="ExternalInput").ap()
    o_xn = nc.dram_tensor("o_xn", [E, TOK], BF16, kind="ExternalOutput").ap()
    with ExitStack() as st:
        sems, Em = _scaffold(nc, st)
        sb = {}
        _alloc(nc, st, sb, [("x_sb", [128, NE * TOK], BF16), ("sq", [128, TOK], BF16),
                            ("ones_col", [128, 1], BF16), ("ones_row", [1, 128], BF16),
                            ("mrow", [1, 2 * TOK], F32), ("eps", [1, 1], F32),
                            ("mrow_bf", [1, 2 * TOK], BF16), ("mB", [128, 2 * TOK], BF16),
                            ("xn_t", [128, 2 * TOK], BF16)])
        _alloc(nc, st, sb, [("ps_stat", [1, 2 * TOK], F32),
                            ("ps_bcast", [128, 2 * TOK], F32)], psum=True)
        sb["ps_b"] = sb["ps_bcast"]
        d, dg = sems["dma"], sems["dg"]

        @Em.gpsimd
        def _(g):
            g.memset(sb["ones_col"][:, :], 1.0)
            g.memset(sb["ones_row"][:, :], 1.0)
            sems["gp"].inc(g.memset(sb["eps"][:, :], 1e-5))
            xr = xT.rearrange("(t p) m -> t p m", p=128)
            for t in range(NE):
                dg.inc(g.dma_start(out=sb["x_sb"][:, t * TOK:(t + 1) * TOK], in_=xr[t]), 16)
        ld, lc = dg.n, sems["gp"].n

        def fw(e):
            e.wait_ge(dg.h, ld)
            e.wait_ge(sems["gp"].h, lc)
        Em.scalar(fw); Em.tensor(fw); Em.vector(fw)
        emit_ln(nc, Em, sems, sb, sb["x_sb"], o_xn)
        _final_wait(Em, sems)
    return nc


def build_A2():
    nc = bass.Bass("TRN2", target_bir_lowering=False)
    g_xn = nc.dram_tensor("g_xn", [NCORE * E, TOK], BF16, kind="ExternalInput").ap()
    io = {}
    for nm, shp, dt in [("w_qT", [E, 512], F32), ("w_kT", [E, 128], F32),
                        ("w_vT", [E, 128], F32), ("w_gT", [E, 8], F32),
                        ("w_lA", [E, 128], F32), ("ex8", [8, 128], BF16),
                        ("cossd", [S, D], BF16), ("sinsd", [S, D], BF16),
                        ("idm", [128, 128], BF16), ("idmf", [128, 128], F32)]:
        io[nm] = nc.dram_tensor(nm, shp, dt, kind="ExternalInput").ap()
    io["i_xnown"] = nc.dram_tensor("i_xnown", [E, TOK], BF16, kind="ExternalInput").ap()
    o_qsp = nc.dram_tensor("o_qsp", [8 * 128, S], BF16, kind="ExternalOutput").ap()
    o_krT = nc.dram_tensor("o_krT", [128, 2 * S], BF16, kind="ExternalOutput").ap()
    o_vtm = nc.dram_tensor("o_vtm", [128, 2 * NT * 132], BF16, kind="ExternalOutput").ap()
    o_loww = nc.dram_tensor("o_loww", [128, TOK], BF16, kind="ExternalOutput").ap()
    with ExitStack() as st:
        sems, Em = _scaffold(nc, st)
        d, dg, vc = sems["dma"], sems["dg"], sems["vc"]
        sb = {}
        _alloc(nc, st, sb, [("krT", [128, 2 * S], BF16), ("vtm", [128, 2 * NT * 132], BF16),
                            ("idm_sb", [128, 128], BF16), ("idm_f32", [128, 128], F32)])
        sb["idm"] = sb["idm_sb"]

        @Em.gpsimd
        def _(g):
            sems["gp"].inc(g.memset(sb["vtm"][:, :], 1.0))

        @Em.sync
        def _(sy):
            d.inc(sy.dma_start(out=sb["idm_sb"][:, :], in_=io["idm"]), 16)
            d.inc(sy.dma_start(out=sb["idm_f32"][:, :], in_=io["idmf"]), 16)
        with ExitStack() as stA:
            _alloc(nc, stA, sb, [("w_q_sb", [128, NE * 512], BF16),
                                 ("w_k_sb", [128, NE * 128], BF16),
                                 ("w_v_sb", [128, NE * 128], BF16),
                                 ("xng", [128, 2 * 4096], BF16),
                                 ("cossd_sb", [128, NT * 128], BF16),
                                 ("sinsd_sb", [128, NT * 128], BF16),
                                 ("qr_tm", [128, 512], BF16), ("kr_tm", [128, 128], BF16),
                                 ("tr_ev", [128, 512], BF16), ("rtmp", [128, 128], BF16)])
            _alloc(nc, stA, sb, [("ps_q", [128, 512], F32), ("ps_k", [128, 128], F32),
                                 ("ps_v", [128, 128], F32), ("ps_tr", [128, 512], BF16),
                                 ("ps_trk", [128, 128], BF16)], psum=True)
            sb["cossd"], sb["sinsd"] = sb["cossd_sb"], sb["sinsd_sb"]

            @Em.gpsimd
            def _(g):
                _slab_loads(g, dg, [(sb["w_q_sb"], io["w_qT"], 512),
                                    (sb["w_k_sb"], io["w_kT"], 128),
                                    (sb["w_v_sb"], io["w_vT"], 128)])

            @Em.sync
            def _(sy):
                csr = io["cossd"].rearrange("(t p) m -> t p m", p=128)
                ssr = io["sinsd"].rearrange("(t p) m -> t p m", p=128)
                for t in range(NT):
                    d.inc(sy.dma_start(out=sb["cossd_sb"][:, t * 128:(t + 1) * 128], in_=csr[t]), 16)
                    d.inc(sy.dma_start(out=sb["sinsd_sb"][:, t * 128:(t + 1) * 128], in_=ssr[t]), 16)
            pre_d, pre_dg, pre_g = d.n, dg.n, sems["gp"].n

            def fw(e):
                e.wait_ge(d.h, pre_d); e.wait_ge(dg.h, pre_dg); e.wait_ge(sems["gp"].h, pre_g)
            Em.tensor(fw); Em.vector(fw)
            emit_proj0(nc, Em, sems, sb, g_xn, o_qsp)
            kv_done = vc.n

            @Em.sync
            def _(sy):
                sy.wait_ge(vc.h, kv_done)
                sems["ds"].inc(sy.dma_start(out=o_krT, in_=sb["krT"][:, :]), 16)
                sems["ds"].inc(sy.dma_start(out=o_vtm, in_=sb["vtm"][:, :]), 16)
            emit_barrier(nc, Em, sems)
        with ExitStack() as stB:
            _alloc(nc, stB, sb, [("xn_loc", [128, 2 * TOK], BF16),
                                 ("w_g_sb", [128, NE * 8], BF16),
                                 ("w_lA_sb", [128, NE * 128], BF16),
                                 ("gl_sb", [8, TOK], F32), ("glT", [128, 8], F32),
                                 ("mx8", [128, 8], F32), ("wcols", [128, 8], F32),
                                 ("wf_tm", [128, 8], BF16), ("wf_tm2", [128, 8], BF16),
                                 ("wf_fm", [8, TOK], BF16), ("low_sb", [128, TOK], BF16),
                                 ("low_w", [128, TOK], BF16), ("ex8_sb", [8, 128], BF16)])
            _alloc(nc, stB, sb, [("ps_g", [8, TOK], F32), ("ps_low", [128, TOK], F32),
                                 ("ps_glT", [128, 8], F32), ("ps_wfT", [8, 128], BF16),
                                 ("ps_wex", [128, TOK], F32)], psum=True)

            @Em.gpsimd
            def _(g):
                _slab_loads(g, dg, [(sb["w_g_sb"], io["w_gT"], 8),
                                    (sb["w_lA_sb"], io["w_lA"], 128)])
                dg.inc(g.dma_start(out=sb["ex8_sb"][:, :], in_=io["ex8"]), 16)
                sb["ex8"] = sb["ex8_sb"]
            pre_dg = dg.n

            def fw2(e):
                e.wait_ge(dg.h, pre_dg)
            Em.tensor(fw2); Em.vector(fw2)
            low_done = emit_gate_low(nc, Em, sems, sb, io["i_xnown"])
            lw = vc.n

            @Em.sync
            def _(sy):
                sy.wait_ge(vc.h, lw)
                sems["ds"].inc(sy.dma_start(out=o_loww, in_=sb["low_w"][:, :]), 16)
            emit_barrier(nc, Em, sems)
        _final_wait(Em, sems)
    return nc



def build_B():
    nc = bass.Bass("TRN2", target_bir_lowering=False)
    io = {}
    for nm, shp, dt in [("i_qsp", [8 * 128, S], BF16), ("i_krT", [128, 2 * S], BF16),
                        ("i_vtm", [128, 2 * NT * 132], BF16),
                        ("hp1", [512, 128], F32), ("hp2", [512, 128], F32),
                        ("hb1", [128, 4], F32), ("hb2", [128, 4], F32),
                        ("idm", [128, 128], BF16), ("triu", [128, 128], BF16),
                        ("triu01", [128, 128], BF16)]:
        io[nm] = nc.dram_tensor(nm, shp, dt, kind="ExternalInput").ap()
    o_oslab = nc.dram_tensor("o_oslab", [8 * 128, S], BF16, kind="ExternalOutput").ap()
    with ExitStack() as st:
        sems, Em = _scaffold(nc, st)
        d, dg, vc = sems["dma"], sems["dg"], sems["vc"]
        sb = {}
        _alloc(nc, st, sb, [("krT", [128, 2 * S], BF16), ("vtm", [128, 2 * NT * 132], BF16),
                            ("idm_sb", [128, 128], BF16), ("triu_sb", [128, 128], BF16),
                            ("triu01", [128, 128], BF16), ("nhb1", [128, 4], F32),
                            ("hp1_sb", [128, 512], BF16), ("hp2_sb", [128, 512], BF16),
                            ("hb1_sb", [128, 4], F32), ("hb2_sb", [128, 4], F32),
                            ("qr_p", [128, S], BF16), ("qh_p", [128, S], BF16),
                            ("kh_p", [128, S], BF16), ("qr_pB", [128, S], BF16),
                            ("qh_pB", [128, S], BF16), ("kh_pB", [128, S], BF16),
                            ("oslabB", [128, S], BF16), ("h1_sb", [128, S], BF16),
                            ("tf32", [128, 1024], F32), ("sim_sb", [128, 4 * S], BF16),
                            ("E_sb", [128, 4 * S], BF16), ("kg_sb", [128, S], BF16),
                            ("ieq_sb", [128, S], FP16), ("kt_sb", [128, S], BF16),
                            ("p_sb", [128, S], FP16), ("atT_ev", [128, 2 * 512], BF16),
                            ("onrm", [128, 128], BF16), ("oslab", [128, S], BF16),
                            ("svec", [128, 16], F32), ("smallv", [128, 16], F32),
                            ("zer1", [128, 1], FP16)])
        sb["idm"] = sb["idm_sb"]
        _alloc(nc, st, sb, [("ps_sim0", [128, 512], F32),
                            ("ps_sim1", [128, 512], F32),
                            ("ps_sc0", [128, 512], F32), ("ps_sc1", [128, 512], F32),
                            ("ps_tr4", [128, 512], BF16), ("ps_av0", [128, 132], F32),
                            ("ps_av1", [128, 132], F32)],
               psum=True)
        sb.update(attn_sb=sb["kg_sb"], prb_scr=sb["h1_sb"],
                  dvec=sb["smallv"][:, 14:15])

        @Em.gpsimd
        def _(g):
            sems["gp"].inc(g.memset(sb["zer1"][:, :], 0.0))
            dg.inc(g.dma_start(out=sb["hp1_sb"][:, :].rearrange("p (h e) -> p h e", e=128),
                               in_=io["hp1"].rearrange("(h p) e -> p h e", p=128)), 16)
            dg.inc(g.dma_start(out=sb["hp2_sb"][:, :].rearrange("p (h e) -> p h e", e=128),
                               in_=io["hp2"].rearrange("(h p) e -> p h e", p=128)), 16)
            dg.inc(g.dma_start(out=sb["hb1_sb"][:, :], in_=io["hb1"]), 16)
            dg.inc(g.dma_start(out=sb["hb2_sb"][:, :], in_=io["hb2"]), 16)

        @Em.sync
        def _(sy):
            d.inc(sy.dma_start(out=sb["idm_sb"][:, :], in_=io["idm"]), 16)
            d.inc(sy.dma_start(out=sb["triu_sb"][:, :], in_=io["triu"]), 16)
            d.inc(sy.dma_start(out=sb["triu01"][:, :], in_=io["triu01"]), 16)
            d.inc(sy.dma_start(out=sb["krT"][:, :], in_=io["i_krT"]), 16)
            d.inc(sy.dma_start(out=sb["vtm"][:, :], in_=io["i_vtm"]), 16)
        pre_d, pre_dg, pre_g = d.n, dg.n, sems["gp"].n

        def fw(e):
            e.wait_ge(d.h, pre_d); e.wait_ge(dg.h, pre_dg); e.wait_ge(sems["gp"].h, pre_g)
        Em.tensor(fw); Em.vector(fw); Em.scalar(fw)

        @Em.vector
        def _(v):
            sems["vc"].inc(nc.vector.tensor_scalar(sb["nhb1"][:, :], sb["hb1_sb"][:, :],
                                                   -1.0, None, op0=Op.mult))
        nh_ok = sems["vc"].n

        @Em.scalar
        def _(s_):
            s_.wait_ge(sems["vc"].h, nh_ok)

        def sink(p, oslab_sl, qtr, done):
            @Em.sync
            def _(sy, p=p, qtr=qtr, done=done):
                sy.wait_ge(sems["vc"].h, done)
                sems["ds"].inc(sy.dma_start(
                    out=o_oslab[p * 128:(p + 1) * 128, qtr * TOK:(qtr + 1) * TOK],
                    in_=oslab_sl), 16)
            fin = sems["ds"].n

            @Em.vector
            def _(v_, fin=fin):
                v_.wait_ge(sems["ds"].h, fin)
        emit_pairs_l0(nc, Em, sems, sb, io["i_qsp"], sink, npairs=8)
        _final_wait(Em, sems)
    return nc


def build_C1():
    nc = bass.Bass("TRN2", target_bir_lowering=False)
    io = {}
    for nm, shp, dt in [("i_at", [E, TOK], BF16), ("i_xn", [E, TOK], BF16),
                        ("i_loww", [128, TOK], BF16), ("xT", [E, TOK], F32),
                        ("w_oT", [E, E], F32), ("w_bT", [E, E], F32),
                        ("w_lB", [128, E], F32), ("b_b", [128, NE], F32)]:
        io[nm] = nc.dram_tensor(nm, shp, dt, kind="ExternalInput").ap()
    o_x1 = nc.dram_tensor("o_x1", [E, TOK], BF16, kind="ExternalOutput").ap()
    with ExitStack() as st:
        sems, Em = _scaffold(nc, st)
        d, dg, vc = sems["dma"], sems["dg"], sems["vc"]
        sb = {"w_oT_h": io["w_oT"], "w_bT_h": io["w_bT"]}
        _alloc(nc, st, sb, [("at_sb", [128, NE * TOK], BF16), ("xn_sb", [128, NE * TOK], BF16),
                            ("wo_sl", [128, 2 * 4096], BF16), ("wb_sl", [128, 2 * 4096], BF16),
                            ("w_lB_sb", [128, 4096], BF16), ("low_w", [128, TOK], BF16),
                            ("bb_sb", [128, NE], F32), ("xt_f", [128, 2 * TOK], F32),
                            ("ev_f", [128, TOK], F32), ("x1_sb", [128, 2 * TOK], BF16)])
        _alloc(nc, st, sb, [("ps_e0", [128, 512], F32), ("ps_e1", [128, 512], F32)],
               psum=True)

        @Em.sync
        def _(sy):
            atr = io["i_at"].rearrange("(t p) m -> t p m", p=128)
            xnr = io["i_xn"].rearrange("(t p) m -> t p m", p=128)
            for t in range(NE):
                d.inc(sy.dma_start(out=sb["at_sb"][:, t * TOK:(t + 1) * TOK], in_=atr[t]), 16)
                d.inc(sy.dma_start(out=sb["xn_sb"][:, t * TOK:(t + 1) * TOK], in_=xnr[t]), 16)
            d.inc(sy.dma_start(out=sb["low_w"][:, :], in_=io["i_loww"]), 16)
            d.inc(sy.dma_start(out=sb["bb_sb"][:, :], in_=io["b_b"]), 16)

        @Em.gpsimd
        def _(g):
            dg.inc(g.dma_start(out=sb["w_lB_sb"][:, :], in_=io["w_lB"]), 16)
        pre_d, pre_dg = d.n, dg.n

        def fw(e):
            e.wait_ge(d.h, pre_d); e.wait_ge(dg.h, pre_dg)
        Em.tensor(fw); Em.vector(fw); Em.scalar(fw)
        emit_evac_l0(nc, Em, sems, sb, None, None, io["xT"], o_x1, 0)
        _final_wait(Em, sems)
    return nc


def build_D():
    nc = bass.Bass("TRN2", target_bir_lowering=False)
    io = {}
    for nm, shp, dt in [("g_xn", [NCORE * E, TOK], BF16), ("i_xnown", [E, TOK], BF16),
                        ("w1_qT", [E, 512], F32), ("w1_kT", [E, 512], F32),
                        ("w1_vT", [E, 512], F32), ("w1_V", [E, 1024], F32),
                        ("w1_S", [128, 8], F32), ("cossd", [S, D], BF16),
                        ("sinsd", [S, D], BF16), ("idm", [128, 128], BF16)]:
        io[nm] = nc.dram_tensor(nm, shp, dt, kind="ExternalInput").ap()
    o_q1 = nc.dram_tensor("o_q1", [8 * 128, S], BF16, kind="ExternalOutput").ap()
    o_k1 = nc.dram_tensor("o_k1", [8 * 128, S], BF16, kind="ExternalOutput").ap()
    o_v1 = nc.dram_tensor("o_v1", [2 * S, 512], BF16, kind="ExternalOutput").ap()
    o_h1 = nc.dram_tensor("o_h1", [8 * 128, TOK], BF16, kind="ExternalOutput").ap()
    with ExitStack() as st:
        sems, Em = _scaffold(nc, st)
        d, dg, vc = sems["dma"], sems["dg"], sems["vc"]
        sb = {}
        _alloc(nc, st, sb, [("idm_sb", [128, 128], BF16)])
        sb["idm"] = sb["idm_sb"]

        @Em.sync
        def _(sy):
            d.inc(sy.dma_start(out=sb["idm_sb"][:, :], in_=io["idm"]), 16)
        with ExitStack() as stA:
            _alloc(nc, stA, sb, [("w_q_sb", [128, NE * 512], BF16),
                                 ("w_k_sb", [128, NE * 512], BF16),
                                 ("w_v_sb", [128, NE * 512], BF16),
                                 ("xng", [128, 2 * 4096], BF16),
                                 ("cossd_sb", [128, NT * 128], BF16),
                                 ("sinsd_sb", [128, NT * 128], BF16),
                                 ("qr_tm", [128, 512], BF16), ("kr_tm", [128, 512], BF16),
                                 ("tr_ev", [128, 512], BF16), ("rtmp", [128, 128], BF16),
                                 ("v_ev", [128, 512], BF16)])
            _alloc(nc, stA, sb, [("ps_q", [128, 512], F32), ("ps_k", [128, 512], F32),
                                 ("ps_v", [128, 512], F32), ("ps_tr", [128, 512], BF16)],
                   psum=True)
            sb["cossd"], sb["sinsd"] = sb["cossd_sb"], sb["sinsd_sb"]

            @Em.gpsimd
            def _(g):
                _slab_loads(g, dg, [(sb["w_q_sb"], io["w1_qT"], 512),
                                    (sb["w_k_sb"], io["w1_kT"], 512),
                                    (sb["w_v_sb"], io["w1_vT"], 512)])

            @Em.sync
            def _(sy):
                csr = io["cossd"].rearrange("(t p) m -> t p m", p=128)
                ssr = io["sinsd"].rearrange("(t p) m -> t p m", p=128)
                for t in range(NT):
                    d.inc(sy.dma_start(out=sb["cossd_sb"][:, t * 128:(t + 1) * 128], in_=csr[t]), 16)
                    d.inc(sy.dma_start(out=sb["sinsd_sb"][:, t * 128:(t + 1) * 128], in_=ssr[t]), 16)
            pre_d, pre_dg = d.n, dg.n

            def fw(e):
                e.wait_ge(d.h, pre_d); e.wait_ge(dg.h, pre_dg)
            Em.tensor(fw); Em.vector(fw)
            emit_proj1(nc, Em, sems, sb, io["g_xn"], o_q1, o_k1, o_v1)
            emit_barrier(nc, Em, sems)
        with ExitStack() as stB:
            _alloc(nc, stB, sb, [("xn_loc32", [128, NE * TOK], BF16),
                                 ("V_sb", [128, NE * 1024], BF16),
                                 ("s1_sb", [128, 8], F32), ("h1_ev", [128, TOK], BF16)])
            _alloc(nc, stB, sb, [("ps_e0", [128, 512], F32), ("ps_e1", [128, 512], F32)],
                   psum=True)

            @Em.gpsimd
            def _(g):
                _slab_loads(g, dg, [(sb["V_sb"], io["w1_V"], 1024)])

            @Em.sync
            def _(sy):
                d.inc(sy.dma_start(out=sb["s1_sb"][:, :], in_=io["w1_S"]), 16)
            pre_d, pre_dg = d.n, dg.n

            def fw2(e):
                e.wait_ge(d.h, pre_d); e.wait_ge(dg.h, pre_dg)
            Em.tensor(fw2); Em.vector(fw2); Em.scalar(fw2)
            emit_ffn1_h(nc, Em, sems, sb, io["i_xnown"], o_h1)
            emit_barrier(nc, Em, sems)
        _final_wait(Em, sems)
    return nc


def build_E():
    nc = bass.Bass("TRN2", target_bir_lowering=False)
    io = {}
    for nm, shp, dt in [("i_q1", [8 * 128, S], BF16), ("i_k1", [8 * 128, S], BF16),
                        ("i_v1", [2 * S, 512], BF16), ("idm", [128, 128], BF16)]:
        io[nm] = nc.dram_tensor(nm, shp, dt, kind="ExternalInput").ap()
    o_oslab = nc.dram_tensor("o_oslab", [8 * 128, S], BF16, kind="ExternalOutput").ap()
    with ExitStack() as st:
        sems, Em = _scaffold(nc, st)
        d, dg, tn, vc = sems["dma"], sems["dg"], sems["tns"], sems["vc"]
        sb = {}
        _alloc(nc, st, sb, [("idm_sb", [128, 128], BF16), ("q1_p", [128, S], BF16),
                            ("k1_p", [128, S], BF16), ("vtm", [128, NT * 132], BF16),
                            ("atT_ev", [128, 4 * 1024], BF16), ("onrm", [128, 128], BF16),
                            ("oslab", [128, S], BF16), ("smallv", [128, 8], F32)])
        _alloc(nc, st, sb, [("ps_sc0", [128, 1024], F32), ("ps_sc1", [128, 1024], F32),
                            ("ps_av0", [128, 396], F32), ("ps_av1", [128, 396], F32),
                            ("ps_av2", [128, 396], F32),
                            ("ps_tr4", [128, 512], BF16)], psum=True)
        sb["dvec"] = sb["smallv"][:, 6:7]
        sb["idm"] = sb["idm_sb"]

        @Em.gpsimd
        def _(g):
            sems["gp"].inc(g.memset(sb["vtm"][:, :], 1.0))

        @Em.sync
        def _(sy):
            d.inc(sy.dma_start(out=sb["idm_sb"][:, :], in_=io["idm"]), 16)
        pre_d, pre_g = d.n, sems["gp"].n

        def fw(e):
            e.wait_ge(d.h, pre_d); e.wait_ge(sems["gp"].h, pre_g)
        Em.tensor(fw); Em.vector(fw); Em.scalar(fw)
        for p in range(8):
            b, hi = p // 4, p % 4

            @Em.sync
            def _(sy, p=p, b=b, hi=hi):
                sy.wait_ge(tn.h, sems.get("_a1", {}).get(("pairend",), 0))
                qr = io["i_q1"].rearrange("(pair q) m -> pair q m", q=128)
                kr = io["i_k1"].rearrange("(pair q) m -> pair q m", q=128)
                d.inc(sy.dma_start(out=sb["q1_p"][:, :], in_=qr[p]), 16)
                d.inc(sy.dma_start(out=sb["k1_p"][:, :], in_=kr[p]), 16)
                vsr = io["i_v1"].rearrange("(t p2) m -> t p2 m", p2=128)
                d.inc(sy.dma_start(
                    out=sb["vtm"][:, :].rearrange("p2 (t m) -> p2 t m", m=132)[:, :, 0:128],
                    in_=vsr[b * NT:(b + 1) * NT, :, hi * 128:(hi + 1) * 128].rearrange(
                        "t p2 m -> p2 t m")), 16)
            ld_ok = d.n

            @Em.tensor
            def _(te, ld_ok=ld_ok):
                te.wait_ge(d.h, ld_ok)
            emit_attn1_pair(nc, Em, sems, sb, b, hi, sb["oslab"])
            sems.setdefault("_a1", {})[("pairend",)] = tn.n
            done = vc.n

            @Em.sync
            def _(sy, p=p, done=done):
                sy.wait_ge(vc.h, done)
                sems["ds"].inc(sy.dma_start(out=o_oslab[p * 128:(p + 1) * 128, :],
                                            in_=sb["oslab"][:, :]), 16)
            fin = sems["ds"].n

            @Em.vector
            def _(v, fin=fin):
                v.wait_ge(sems["ds"].h, fin)
        _final_wait(Em, sems)
    return nc


def build_F():
    nc = bass.Bass("TRN2", target_bir_lowering=False)
    io = {}
    for nm, shp, dt in [("i_at", [E, TOK], BF16), ("i_h1", [8 * 128, TOK], BF16),
                        ("i_x1", [E, TOK], BF16), ("w1_oT", [E, E], F32),
                        ("w1_UT", [1024, E], F32)]:
        io[nm] = nc.dram_tensor(nm, shp, dt, kind="ExternalInput").ap()
    o_x = nc.dram_tensor("o_x", [E, TOK], F32, kind="ExternalOutput").ap()
    with ExitStack() as st:
        sems, Em = _scaffold(nc, st)
        d, dg, vc = sems["dma"], sems["dg"], sems["vc"]
        sb = {"w1_oT_h": io["w1_oT"], "w1_UT_h": io["w1_UT"]}
        _alloc(nc, st, sb, [("at_sb", [128, NE * TOK], BF16),
                            ("h1_sb2", [128, 8 * TOK], BF16),
                            ("wo_sl", [128, 2 * 4096], BF16), ("wu_sl", [128, 2 * 1024], BF16),
                            ("xt_f", [128, 2 * TOK], BF16), ("ev_f", [128, TOK], F32)])
        _alloc(nc, st, sb, [("ps_e0", [128, 512], F32), ("ps_e1", [128, 512], F32)],
               psum=True)

        @Em.sync
        def _(sy):
            atr = io["i_at"].rearrange("(t p) m -> t p m", p=128)
            for t in range(NE):
                d.inc(sy.dma_start(out=sb["at_sb"][:, t * TOK:(t + 1) * TOK], in_=atr[t]), 16)
            h1r = io["i_h1"].rearrange("(t p) m -> t p m", p=128)
            for t in range(8):
                d.inc(sy.dma_start(out=sb["h1_sb2"][:, t * TOK:(t + 1) * TOK], in_=h1r[t]), 16)
        pre_d = d.n

        def fw(e):
            e.wait_ge(d.h, pre_d)
        Em.tensor(fw); Em.vector(fw); Em.scalar(fw)
        emit_evac_l1(nc, Em, sems, sb, io["i_x1"], o_x)
        _final_wait(Em, sems)
    return nc


_BUILD_CACHE = {}


def _run(bname, build_fn, in_maps, **kw):
    if bname not in _BUILD_CACHE:
        _BUILD_CACHE[bname] = build_fn()
    return run_bass_kernel_spmd(_BUILD_CACHE[bname], in_maps, core_ids=list(range(NCORE)))


def kernel_unfused(**inputs):
    cs = host_prepare(inputs)
    rA = _run("ln0", lambda: build_ln(F32), [{"xT": c["xT"]} for c in cs])
    xn_sh = [np.asarray(rA.results[c]["o_xn"]) for c in range(NCORE)]
    g_xn = np.concatenate(xn_sh, 0)
    keysA2 = ["w_qT", "w_kT", "w_vT", "w_gT", "w_lA", "ex8", "cossd", "sinsd", "idm"]
    rA2 = _run("A2", build_A2, [dict(g_xn=g_xn, i_xnown=xn_sh[c], idmf=np.eye(128, dtype=np.float32),
                                     **{k: cs[c][k] for k in keysA2}) for c in range(NCORE)])
    keysB = ["hp1", "hp2", "hb1", "hb2", "idm", "triu"]
    rB = _run("B", build_B, [dict(i_qsp=np.asarray(rA2.results[c]["o_qsp"]),
                                  i_krT=np.asarray(rA2.results[c]["o_krT"]),
                                  i_vtm=np.asarray(rA2.results[c]["o_vtm"]),
                                  **{k: cs[c][k] for k in keysB}) for c in range(NCORE)])
    at = []
    for c in range(NCORE):
        bc = c // 4
        t0 = (c % 4) * TOK
        m = np.zeros((E, TOK), ml_dtypes.bfloat16)
        for h in range(H):
            src_core = h // 4
            p = bc * 4 + (h % 4)
            m[h * 128:(h + 1) * 128, :] = np.asarray(
                rB.results[src_core]["o_oslab"])[p * 128:(p + 1) * 128, t0:t0 + TOK]
        at.append(m)
    keysC = ["xT", "w_oT", "w_bT", "w_lB", "b_b"]
    rC = _run("C1", build_C1, [dict(i_at=at[c], i_xn=xn_sh[c],
                                    i_loww=np.asarray(rA2.results[c]["o_loww"]),
                                    **{k: cs[c][k] for k in keysC}) for c in range(NCORE)])
    x1_sh = [np.asarray(rC.results[c]["o_x1"]) for c in range(NCORE)]
    rC2 = _run("ln1", lambda: build_ln(BF16), [{"xT": x1_sh[c]} for c in range(NCORE)])
    xn1_sh = [np.asarray(rC2.results[c]["o_xn"]) for c in range(NCORE)]
    g_xn1 = np.concatenate(xn1_sh, 0)
    keysD = ["w1_qT", "w1_kT", "w1_vT", "w1_V", "w1_S", "cossd", "sinsd", "idm"]
    rD = _run("D", build_D, [dict(g_xn=g_xn1, i_xnown=xn1_sh[c],
                                  **{k: cs[c][k] for k in keysD}) for c in range(NCORE)])
    rE = _run("E", build_E, [dict(i_q1=np.asarray(rD.results[c]["o_q1"]),
                                  i_k1=np.asarray(rD.results[c]["o_k1"]),
                                  i_v1=np.asarray(rD.results[c]["o_v1"]),
                                  idm=cs[c]["idm"]) for c in range(NCORE)])
    at1 = []
    for c in range(NCORE):
        bc = c // 4
        t0 = (c % 4) * TOK
        m = np.zeros((E, TOK), ml_dtypes.bfloat16)
        for h in range(H):
            src_core = h // 4
            p = bc * 4 + (h % 4)
            m[h * 128:(h + 1) * 128, :] = np.asarray(
                rE.results[src_core]["o_oslab"])[p * 128:(p + 1) * 128, t0:t0 + TOK]
        at1.append(m)
    rF = _run("F", build_F, [dict(i_at=at1[c], i_h1=np.asarray(rD.results[c]["o_h1"]),
                                  i_x1=x1_sh[c], w1_oT=cs[c]["w1_oT"],
                                  w1_UT=cs[c]["w1_UT"]) for c in range(NCORE)])
    out = np.zeros((B, S, E), np.float32)
    for c in range(NCORE):
        bc = c // 4
        t0 = (c % 4) * TOK
        out[bc, t0:t0 + TOK, :] = np.asarray(rF.results[c]["o_x"]).T
    return out



def build_fused():
    nc = bass.Bass("TRN2", target_bir_lowering=False)
    io = {}
    for nm, shp, dt in [
            ("xT", [E, TOK], F32),
            ("w_qT", [E, 512], F32), ("w_kT", [E, 128], F32), ("w_vT", [E, 128], F32),
            ("w_gT", [E, 8], F32), ("w_lA", [E, 128], F32), ("ex8", [8, 128], BF16),
            ("hp1", [512, 128], F32), ("hp2", [512, 128], F32),
            ("hb1", [128, 4], F32), ("hb2", [128, 4], F32),
            ("w_oT", [E, E], F32), ("w_bT", [E, E], F32), ("w_lB", [128, E], F32),
            ("b_b", [128, NE], F32),
            ("w1_qT", [E, 512], F32), ("w1_kT", [E, 512], F32), ("w1_vT", [E, 512], F32),
            ("w1_V", [E, 1024], F32), ("w1_S", [128, 8], F32),
            ("w1_oT", [E, E], F32), ("w1_UT", [1024, E], F32),
            ("cossd", [S, D], BF16), ("sinsd", [S, D], BF16),
            ("idm", [128, 128], BF16), ("idmf", [128, 128], F32),
            ("triu", [128, 128], BF16), ("triu01", [128, 128], BF16)]:
        io[nm] = nc.dram_tensor(nm, shp, dt, kind="ExternalInput").ap()
    o_x = nc.dram_tensor("o_x", [E, TOK], F32, kind="ExternalOutput").ap()
    b_xn_h = nc.dram_tensor("b_xn", [E, TOK], BF16)
    g_xn_h = nc.dram_tensor("g_xn_i", [NCORE * E, TOK], BF16)
    qsp = nc.dram_tensor("qsp", [8 * 128, S], BF16)
    a2a_in_h = nc.dram_tensor("a2a_in", [E, TOK], BF16)
    a2a_out_h = nc.dram_tensor("a2a_out", [E, TOK], BF16)
    x1_h = nc.dram_tensor("x1_i", [E, TOK], BF16)
    q1sp = nc.dram_tensor("q1sp", [8 * 128, S], BF16)
    k1sp = nc.dram_tensor("k1sp", [8 * 128, S], BF16)
    v1sp = nc.dram_tensor("v1sp", [2 * S, 512], BF16)
    h1_h = nc.dram_tensor("h1_i", [8 * 128, TOK], BF16)
    b_xn, g_xn = b_xn_h.ap(), g_xn_h.ap()
    a2a_in, a2a_out = a2a_in_h.ap(), a2a_out_h.ap()

    with ExitStack() as st:
        sems, Em = _scaffold(nc, st)
        d, dg, tn, sc, vc, cc = (sems["dma"], sems["dg"], sems["tns"], sems["sc"],
                                 sems["vc"], sems["cc"])
        sb = {"w_oT_h": io["w_oT"], "w_bT_h": io["w_bT"],
              "w1_oT_h": io["w1_oT"], "w1_UT_h": io["w1_UT"]}
        _alloc(nc, st, sb, [("idm_sb", [128, 128], BF16), ("idm_f32", [128, 128], F32),
                            ("triu_sb", [128, 128], BF16), ("low_w", [128, TOK], BF16),
                            ("zer1", [128, 1], FP16)])
        sb["idm"] = sb["idm_sb"]

        @Em.sync
        def _(sy):
            d.inc(sy.dma_start(out=sb["idm_sb"][:, :], in_=io["idm"]), 16)
            d.inc(sy.dma_start(out=sb["idm_f32"][:, :], in_=io["idmf"]), 16)
            d.inc(sy.dma_start(out=sb["triu_sb"][:, :], in_=io["triu"]), 16)

        @Em.gpsimd
        def _(g):
            sems["gp"].inc(g.memset(sb["zer1"][:, :], 0.0))

        def ln_phase(src_hbm, src_dt):
            with ExitStack() as stL:
                sbL = dict(sb)
                _alloc(nc, stL, sbL, [("x_sbL", [128, NE * TOK], BF16),
                                      ("sq", [128, TOK], BF16),
                                      ("ones_col", [128, 1], BF16),
                                      ("ones_row", [1, 128], BF16),
                                      ("mrow", [1, 2 * TOK], F32), ("eps", [1, 1], F32),
                                      ("mrow_bf", [1, 2 * TOK], BF16),
                                      ("mB", [128, 2 * TOK], BF16),
                                      ("xn_t", [128, 2 * TOK], BF16)])
                _alloc(nc, stL, sbL, [("ps_stat", [1, 2 * TOK], F32),
                                      ("ps_bcast", [128, 2 * TOK], F32)], psum=True)

                @Em.gpsimd
                def _(g):
                    g.memset(sbL["ones_col"][:, :], 1.0)
                    g.memset(sbL["ones_row"][:, :], 1.0)
                    sems["gp"].inc(g.memset(sbL["eps"][:, :], 1e-5))
                    xr = src_hbm.rearrange("(t p) m -> t p m", p=128)
                    for t in range(NE):
                        dg.inc(g.dma_start(out=sbL["x_sbL"][:, t * TOK:(t + 1) * TOK],
                                           in_=xr[t]), 16)
                ld, lc = dg.n, sems["gp"].n

                def fw(e):
                    e.wait_ge(dg.h, ld)
                    e.wait_ge(sems["gp"].h, lc)
                Em.scalar(fw); Em.tensor(fw); Em.vector(fw)
                emit_ln(nc, Em, sems, sbL, sbL["x_sbL"], b_xn)
                emit_barrier(nc, Em, sems)
            emit_gather(nc, Em, sems, b_xn_h, g_xn_h, d.n)
            emit_barrier(nc, Em, sems)

        ln_phase(io["xT"], F32)
        with ExitStack() as stKV:
            _alloc(nc, stKV, sb, [("krT", [128, 2 * S], BF16),
                                  ("vtm", [128, 2 * NT * 132], BF16)])

            @Em.gpsimd
            def _(g):
                sems["gp"].inc(g.memset(sb["vtm"][:, :], 1.0))
            with ExitStack() as stP:
                _alloc(nc, stP, sb, [("w_q_sb", [128, NE * 512], BF16),
                                     ("w_k_sb", [128, NE * 128], BF16),
                                     ("w_v_sb", [128, NE * 128], BF16),
                                     ("xng", [128, 2 * 4096], BF16),
                                     ("cossd_sb", [128, NT * 128], BF16),
                                     ("sinsd_sb", [128, NT * 128], BF16),
                                     ("qr_tm", [128, 512], BF16), ("kr_tm", [128, 128], BF16),
                                     ("tr_ev", [128, 512], BF16), ("rtmp", [128, 128], BF16)])
                _alloc(nc, stP, sb, [("ps_q", [128, 512], F32), ("ps_k", [128, 128], F32),
                                     ("ps_v", [128, 128], F32), ("ps_tr", [128, 512], BF16),
                                     ("ps_trk", [128, 128], BF16)], psum=True)
                sb["cossd"], sb["sinsd"] = sb["cossd_sb"], sb["sinsd_sb"]

                @Em.gpsimd
                def _(g):
                    _slab_loads(g, dg, [(sb["w_q_sb"], io["w_qT"], 512),
                                        (sb["w_k_sb"], io["w_kT"], 128),
                                        (sb["w_v_sb"], io["w_vT"], 128)])

                @Em.sync
                def _(sy):
                    csr = io["cossd"].rearrange("(t p) m -> t p m", p=128)
                    ssr = io["sinsd"].rearrange("(t p) m -> t p m", p=128)
                    for t in range(NT):
                        d.inc(sy.dma_start(out=sb["cossd_sb"][:, t * 128:(t + 1) * 128], in_=csr[t]), 16)
                        d.inc(sy.dma_start(out=sb["sinsd_sb"][:, t * 128:(t + 1) * 128], in_=ssr[t]), 16)
                pre = (d.n, dg.n, sems["gp"].n)

                def fw(e):
                    e.wait_ge(d.h, pre[0]); e.wait_ge(dg.h, pre[1]); e.wait_ge(sems["gp"].h, pre[2])
                Em.tensor(fw); Em.vector(fw)
                emit_proj0(nc, Em, sems, sb, g_xn, qsp)
                emit_barrier(nc, Em, sems)
            with ExitStack() as stG:
                _alloc(nc, stG, sb, [("xn_loc", [128, 2 * TOK], BF16),
                                     ("w_g_sb", [128, NE * 8], BF16),
                                     ("w_lA_sb", [128, NE * 128], BF16),
                                     ("gl_sb", [8, TOK], F32), ("glT", [128, 8], F32),
                                     ("mx8", [128, 8], F32), ("wcols", [128, 8], F32),
                                     ("wf_tm", [128, 8], BF16), ("wf_tm2", [128, 8], BF16),
                                     ("wf_fm", [8, TOK], BF16), ("low_sb", [128, TOK], BF16),
                                     ("ex8_sb", [8, 128], BF16)])
                _alloc(nc, stG, sb, [("ps_g", [8, TOK], F32), ("ps_low", [128, TOK], F32),
                                     ("ps_glT", [128, 8], F32), ("ps_wfT", [8, 128], BF16),
                                     ("ps_wex", [128, TOK], F32)], psum=True)
                sb["ex8"] = sb["ex8_sb"]

                @Em.gpsimd
                def _(g):
                    _slab_loads(g, dg, [(sb["w_g_sb"], io["w_gT"], 8),
                                        (sb["w_lA_sb"], io["w_lA"], 128)])
                    dg.inc(g.dma_start(out=sb["ex8_sb"][:, :], in_=io["ex8"]), 16)
                pre = dg.n

                def fw2(e):
                    e.wait_ge(dg.h, pre)
                Em.tensor(fw2); Em.vector(fw2)
                emit_gate_low(nc, Em, sems, sb, b_xn)
                emit_barrier(nc, Em, sems)
            with ExitStack() as stB:
                _alloc(nc, stB, sb, [("hp1_sb", [128, 512], BF16), ("hp2_sb", [128, 512], BF16),
                                     ("hb1_sb", [128, 4], F32), ("hb2_sb", [128, 4], F32),
                                     ("triu01", [128, 128], BF16), ("nhb1", [128, 4], F32),
                                     ("qr_p", [128, S], BF16), ("qh_p", [128, S], BF16),
                                     ("kh_p", [128, S], BF16), ("qr_pB", [128, S], BF16),
                                     ("qh_pB", [128, S], BF16), ("kh_pB", [128, S], BF16),
                                     ("oslabB", [128, 2 * TOK], BF16), ("h1_sb", [128, S], BF16),
                                     ("tf32", [128, 1024], F32), ("sim_sb", [128, 4 * S], BF16),
                                     ("E_sb", [128, 4 * S], BF16), ("kg_sb", [128, S], BF16),
                                     ("ieq_sb", [128, S], FP16), ("kt_sb", [128, S], BF16),
                                     ("p_sb", [128, S], FP16), ("atT_ev", [128, 2 * 512], BF16),
                                     ("onrm", [128, 128], BF16), ("oslab", [128, 2 * TOK], BF16),
                                     ("svec", [128, 16], F32), ("smallv", [128, 16], F32)])
                _alloc(nc, stB, sb, [("ps_sim0", [128, 512], F32),
                                     ("ps_sim1", [128, 512], F32),
                                     ("ps_sc0", [128, 512], F32), ("ps_sc1", [128, 512], F32),
                                     ("ps_tr4", [128, 512], BF16), ("ps_av0", [128, 132], F32),
                                     ("ps_av1", [128, 132], F32)],
                       psum=True)
                sb.update(attn_sb=sb["kg_sb"], prb_scr=sb["h1_sb"],
                          dvec=sb["smallv"][:, 14:15])

                @Em.gpsimd
                def _(g):
                    dg.inc(g.dma_start(out=sb["hp1_sb"][:, :].rearrange("p (h e) -> p h e", e=128),
                                       in_=io["hp1"].rearrange("(h p) e -> p h e", p=128)), 16)
                    dg.inc(g.dma_start(out=sb["hp2_sb"][:, :].rearrange("p (h e) -> p h e", e=128),
                                       in_=io["hp2"].rearrange("(h p) e -> p h e", p=128)), 16)
                    dg.inc(g.dma_start(out=sb["hb1_sb"][:, :], in_=io["hb1"]), 16)
                    dg.inc(g.dma_start(out=sb["hb2_sb"][:, :], in_=io["hb2"]), 16)

                @Em.sync
                def _(sy):
                    d.inc(sy.dma_start(out=sb["triu01"][:, :], in_=io["triu01"]), 16)
                pre = dg.n
                pre_d2 = d.n

                def fw3(e):
                    e.wait_ge(dg.h, pre)
                    e.wait_ge(d.h, pre_d2)
                Em.tensor(fw3); Em.vector(fw3); Em.scalar(fw3)

                @Em.vector
                def _(v):
                    vc.inc(nc.vector.tensor_scalar(sb["nhb1"][:, :], sb["hb1_sb"][:, :],
                                                   -1.0, None, op0=Op.mult))
                nh_ok = vc.n

                @Em.scalar
                def _(s_):
                    s_.wait_ge(vc.h, nh_ok)

                def sink(p, oslab_sl, qtr, done):
                    b_, hi = p // 4, p % 4

                    @Em.sync
                    def _(sy, b_=b_, hi=hi, qtr=qtr, so=done):
                        sy.wait_ge(sems["vc"].h, so)
                        rr = 4 * b_ + qtr
                        sems["ds"].inc(sy.dma_start(
                            out=a2a_in[rr * 512 + hi * 128:rr * 512 + (hi + 1) * 128, :],
                            in_=oslab_sl), 16)
                emit_pairs_l0(nc, Em, sems, sb, qsp, sink, npairs=8)
                emit_barrier(nc, Em, sems)
        emit_alltoall(nc, Em, sems, a2a_in_h, a2a_out_h, d.n)
        emit_barrier(nc, Em, sems)
        with ExitStack() as stE:
            _alloc(nc, stE, sb, [("at_sb", [128, NE * TOK], BF16),
                                 ("xn_sb", [128, NE * TOK], BF16),
                                 ("wo_sl", [128, 2 * 4096], BF16),
                                 ("wb_sl", [128, 2 * 4096], BF16),
                                 ("w_lB_sb", [128, 4096], BF16),
                                 ("bb_sb", [128, NE], F32), ("xt_f", [128, 2 * TOK], F32),
                                 ("ev_f", [128, TOK], F32), ("x1_sb", [128, 2 * TOK], BF16)])
            _alloc(nc, stE, sb, [("ps_e0", [128, 512], F32), ("ps_e1", [128, 512], F32)],
                   psum=True)

            @Em.sync
            def _(sy):
                atr = a2a_out.rearrange("(t p) m -> t p m", p=128)
                xnr = b_xn.rearrange("(t p) m -> t p m", p=128)
                for t in range(NE):
                    d.inc(sy.dma_start(out=sb["at_sb"][:, t * TOK:(t + 1) * TOK], in_=atr[t]), 16)
                    d.inc(sy.dma_start(out=sb["xn_sb"][:, t * TOK:(t + 1) * TOK], in_=xnr[t]), 16)
                d.inc(sy.dma_start(out=sb["bb_sb"][:, :], in_=io["b_b"]), 16)

            @Em.gpsimd
            def _(g):
                dg.inc(g.dma_start(out=sb["w_lB_sb"][:, :], in_=io["w_lB"]), 16)
            pre = (d.n, dg.n)

            def fw4(e):
                e.wait_ge(d.h, pre[0]); e.wait_ge(dg.h, pre[1])
            Em.tensor(fw4); Em.vector(fw4); Em.scalar(fw4)
            emit_evac_l0(nc, Em, sems, sb, None, None, io["xT"], x1_h.ap(), 0)
            emit_barrier(nc, Em, sems)
        ln_phase(x1_h.ap(), BF16)
        with ExitStack() as stP1:
            _alloc(nc, stP1, sb, [("w_q1", [128, NE * 512], BF16),
                                  ("w_k1", [128, NE * 512], BF16),
                                  ("w_v1", [128, NE * 512], BF16),
                                  ("xng1", [128, 2 * 4096], BF16),
                                  ("cossd_sb", [128, NT * 128], BF16),
                                  ("sinsd_sb", [128, NT * 128], BF16),
                                  ("qr_tm1", [128, 512], BF16), ("kr_tm1", [128, 512], BF16),
                                  ("tr_ev1", [128, 512], BF16), ("rtmp1", [128, 128], BF16),
                                  ("v_ev", [128, 512], BF16)])
            _alloc(nc, stP1, sb, [("ps_q", [128, 512], F32), ("ps_k", [128, 512], F32),
                                  ("ps_v", [128, 512], F32), ("ps_tr", [128, 512], BF16)],
                   psum=True)
            sb.update(w_q_sb=sb["w_q1"], w_k_sb=sb["w_k1"], w_v_sb=sb["w_v1"],
                      xng=sb["xng1"], qr_tm=sb["qr_tm1"], kr_tm=sb["kr_tm1"],
                      tr_ev=sb["tr_ev1"], rtmp=sb["rtmp1"],
                      cossd=sb["cossd_sb"], sinsd=sb["sinsd_sb"])

            @Em.gpsimd
            def _(g):
                _slab_loads(g, dg, [(sb["w_q1"], io["w1_qT"], 512),
                                    (sb["w_k1"], io["w1_kT"], 512),
                                    (sb["w_v1"], io["w1_vT"], 512)])

            @Em.sync
            def _(sy):
                csr = io["cossd"].rearrange("(t p) m -> t p m", p=128)
                ssr = io["sinsd"].rearrange("(t p) m -> t p m", p=128)
                for t in range(NT):
                    d.inc(sy.dma_start(out=sb["cossd_sb"][:, t * 128:(t + 1) * 128], in_=csr[t]), 16)
                    d.inc(sy.dma_start(out=sb["sinsd_sb"][:, t * 128:(t + 1) * 128], in_=ssr[t]), 16)
            pre = (d.n, dg.n)

            def fw5(e):
                e.wait_ge(d.h, pre[0]); e.wait_ge(dg.h, pre[1])
            Em.tensor(fw5); Em.vector(fw5)
            emit_proj1(nc, Em, sems, sb, g_xn, q1sp.ap(), k1sp.ap(), v1sp.ap())
            emit_barrier(nc, Em, sems)
        with ExitStack() as stF1:
            _alloc(nc, stF1, sb, [("xn_loc32", [128, NE * TOK], BF16),
                                  ("V_sb", [128, NE * 1024], BF16),
                                  ("s1_sb", [128, 8], F32), ("h1_ev", [128, TOK], BF16)])
            _alloc(nc, stF1, sb, [("ps_e0", [128, 512], F32), ("ps_e1", [128, 512], F32)],
                   psum=True)

            @Em.gpsimd
            def _(g):
                _slab_loads(g, dg, [(sb["V_sb"], io["w1_V"], 1024)])

            @Em.sync
            def _(sy):
                d.inc(sy.dma_start(out=sb["s1_sb"][:, :], in_=io["w1_S"]), 16)
            pre = (d.n, dg.n)

            def fw6(e):
                e.wait_ge(d.h, pre[0]); e.wait_ge(dg.h, pre[1])
            Em.tensor(fw6); Em.vector(fw6)
            emit_ffn1_h(nc, Em, sems, sb, b_xn, h1_h.ap())
            emit_barrier(nc, Em, sems)
        with ExitStack() as stA1:
            _alloc(nc, stA1, sb, [("q1_p", [128, S], BF16), ("k1_p", [128, S], BF16),
                                  ("vtm1", [128, NT * 132], BF16),
                                  ("atT_ev1", [128, 4 * 1024], BF16),
                                  ("onrm1", [128, 128], BF16), ("oslab1", [128, S], BF16),
                                  ("smallv1", [128, 8], F32)])
            _alloc(nc, stA1, sb, [("ps_sc0", [128, 1024], F32), ("ps_sc1", [128, 1024], F32),
                                  ("ps_av0", [128, 396], F32), ("ps_av1", [128, 396], F32),
                                  ("ps_av2", [128, 396], F32),
                                  ("ps_tr4", [128, 512], BF16)], psum=True)
            sb.update(vtm=sb["vtm1"], atT_ev=sb["atT_ev1"], onrm=sb["onrm1"],
                      dvec=sb["smallv1"][:, 6:7])

            @Em.gpsimd
            def _(g):
                sems["gp"].inc(g.memset(sb["vtm1"][:, :], 1.0))
            preg = sems["gp"].n

            def fw7(e):
                e.wait_ge(sems["gp"].h, preg)
            Em.tensor(fw7); Em.vector(fw7); Em.scalar(fw7)
            for p in range(8):
                b_, hi = p // 4, p % 4

                @Em.sync
                def _(sy, p=p, b_=b_, hi=hi):
                    sy.wait_ge(tn.h, sems.get("_a1", {}).get(("pairend",), 0))
                    qr = q1sp.ap().rearrange("(pair q) m -> pair q m", q=128)
                    kr = k1sp.ap().rearrange("(pair q) m -> pair q m", q=128)
                    d.inc(sy.dma_start(out=sb["q1_p"][:, :], in_=qr[p]), 16)
                    d.inc(sy.dma_start(out=sb["k1_p"][:, :], in_=kr[p]), 16)
                    vsr = v1sp.ap().rearrange("(t p2) m -> t p2 m", p2=128)
                    d.inc(sy.dma_start(
                        out=sb["vtm1"][:, :].rearrange("p2 (t m) -> p2 t m", m=132)[:, :, 0:128],
                        in_=vsr[b_ * NT:(b_ + 1) * NT, :, hi * 128:(hi + 1) * 128].rearrange(
                            "t p2 m -> p2 t m")), 16)
                ld_ok = d.n

                @Em.tensor
                def _(te, ld_ok=ld_ok):
                    te.wait_ge(d.h, ld_ok)
                emit_attn1_pair(nc, Em, sems, sb, b_, hi, sb["oslab1"])
                sems.setdefault("_a1", {})[("pairend",)] = tn.n
                done = vc.n

                @Em.sync
                def _(sy, p=p, b_=b_, hi=hi, done=done):
                    sy.wait_ge(vc.h, done)
                    for qtr in range(4):
                        rr = 4 * b_ + qtr
                        sems["ds"].inc(sy.dma_start(
                            out=a2a_in[rr * 512 + hi * 128:rr * 512 + (hi + 1) * 128, :],
                            in_=sb["oslab1"][:, qtr * TOK:(qtr + 1) * TOK]), 16)
                fin = sems["ds"].n

                @Em.vector
                def _(v, fin=fin):
                    v.wait_ge(sems["ds"].h, fin)
            emit_barrier(nc, Em, sems)
        emit_alltoall(nc, Em, sems, a2a_in_h, a2a_out_h, d.n)
        emit_barrier(nc, Em, sems)
        with ExitStack() as stEF:
            _alloc(nc, stEF, sb, [("at1_sb", [128, NE * TOK], BF16),
                                  ("h1_sb2", [128, 8 * TOK], BF16),
                                  ("wo_sl1", [128, 2 * 4096], BF16),
                                  ("wu_sl", [128, 2 * 1024], BF16),
                                  ("xt_f1", [128, 2 * TOK], BF16), ("ev_f1", [128, TOK], F32)])
            _alloc(nc, stEF, sb, [("ps_e0", [128, 512], F32), ("ps_e1", [128, 512], F32)],
                   psum=True)
            sb.update(at_sb=sb["at1_sb"], wo_sl=sb["wo_sl1"], xt_f=sb["xt_f1"],
                      ev_f=sb["ev_f1"])

            @Em.sync
            def _(sy):
                atr = a2a_out.rearrange("(t p) m -> t p m", p=128)
                for t in range(NE):
                    d.inc(sy.dma_start(out=sb["at1_sb"][:, t * TOK:(t + 1) * TOK], in_=atr[t]), 16)
                h1r = h1_h.ap().rearrange("(t p) m -> t p m", p=128)
                for t in range(8):
                    d.inc(sy.dma_start(out=sb["h1_sb2"][:, t * TOK:(t + 1) * TOK], in_=h1r[t]), 16)
            pre = d.n

            def fw8(e):
                e.wait_ge(d.h, pre)
            Em.tensor(fw8); Em.vector(fw8); Em.scalar(fw8)
            emit_evac_l1(nc, Em, sems, sb, x1_h.ap(), o_x)
            emit_barrier(nc, Em, sems)
        _final_wait(Em, sems)
    return nc


def kernel(**inputs):
    cs = host_prepare(inputs)
    if "fused" not in _BUILD_CACHE:
        _BUILD_CACHE["fused"] = build_fused()
    keys = ["xT", "w_qT", "w_kT", "w_vT", "w_gT", "w_lA", "ex8", "hp1", "hp2",
            "hb1", "hb2", "w_oT", "w_bT", "w_lB", "b_b", "w1_qT", "w1_kT", "w1_vT",
            "w1_V", "w1_S", "w1_oT", "w1_UT", "cossd", "sinsd", "idm", "triu",
            "triu01"]
    idmf = np.eye(128, dtype=np.float32)
    in_maps = [dict(idmf=idmf, **{k: cs[c][k] for k in keys}) for c in range(NCORE)]
    res = run_bass_kernel_spmd(_BUILD_CACHE["fused"], in_maps, core_ids=list(range(NCORE)))
    out = np.zeros((B, S, E), np.float32)
    for c in range(NCORE):
        bc = c // 4
        t0 = (c % 4) * TOK
        out[bc, t0:t0 + TOK, :] = np.asarray(res.results[c]["o_x"]).T
    return out


